# version 16
# baseline (speedup 1.0000x reference)
import numpy as np
import concourse.bass as bass
import concourse.mybir as mybir
from concourse.bass_utils import run_bass_kernel_spmd

F32 = mybir.dt.float32
BF16 = mybir.dt.bfloat16
AF = mybir.ActivationFunctionType
ALU = mybir.AluOpType
AX = mybir.AxisListType

NCORE = 8
T = 2048
D = 1024
NTC = 16
F32EPS = float(np.finfo(np.float32).eps)
RSCALE = float(32.0 ** -0.5)



def prep_core_inputs(inputs, c):
    f = np.float32
    ac = np.ascontiguousarray
    d = {}
    d["x"] = ac(np.asarray(inputs["L0_attn_x"])[c].astype(f, copy=False))
    d["wqT0"] = ac(np.asarray(inputs["L0_attn_Wq"]).T.astype(f, copy=False))
    d["wkT0"] = ac(np.asarray(inputs["L0_attn_Wk"]).T.astype(f, copy=False))
    d["wvT0"] = ac(np.asarray(inputs["L0_attn_Wv"]).T.astype(f, copy=False))
    d["wg0"] = ac(np.asarray(inputs["L0_ffn_Wg"]).astype(f, copy=False))
    d["aall"] = ac(np.asarray(inputs["L0_ffn_A"]).transpose(1, 0, 2).reshape(1024, 128).astype(f, copy=False))
    d["ball"] = ac(np.asarray(inputs["L0_ffn_B"]).reshape(128, 1024).astype(f, copy=False))
    d["wq1"] = ac(np.asarray(inputs["L1_attn_Wq"]).astype(f, copy=False))
    d["wk1"] = ac(np.asarray(inputs["L1_attn_Wk"]).astype(f, copy=False))
    d["wv1"] = ac(np.asarray(inputs["L1_attn_Wv"]).astype(f, copy=False))
    d["wgk1"] = ac(np.asarray(inputs["L1_attn_Wgk1"]).astype(f, copy=False))
    d["wgk2"] = ac(np.asarray(inputs["L1_attn_Wgk2"]).astype(f, copy=False))
    d["bgk2t"] = ac(np.broadcast_to(np.asarray(inputs["L1_attn_bgk2"]).astype(f, copy=False), (128, 512)))
    d["wg1"] = ac(np.asarray(inputs["L1_attn_Wg"]).astype(f, copy=False))
    d["gnwt"] = ac(np.broadcast_to(np.tile(np.asarray(inputs["L1_attn_gnw"]).astype(f, copy=False), 16), (128, 1024)))
    d["wo1"] = ac(np.asarray(inputs["L1_attn_Wo"]).astype(f, copy=False))
    d["gwT"] = ac(np.asarray(inputs["L1_ffn_gate_w"]).T.astype(f, copy=False))
    d["gbt"] = ac(np.broadcast_to(np.asarray(inputs["L1_ffn_gate_b"]).astype(f, copy=False), (128, 8)))
    d["rmswt"] = ac(np.broadcast_to(np.asarray(inputs["L1_ffn_rms_w"]).astype(f, copy=False), (128, 8)))
    d["w1"] = ac(np.asarray(inputs["L1_ffn_w1"]).astype(f, copy=False)
                 .reshape(8, 8, 128, 16, 128).transpose(0, 3, 2, 1, 4).reshape(8, 16, 128, 1024))
    d["b1c"] = ac(np.asarray(inputs["L1_ffn_b1"]).reshape(8, 16, 128).transpose(0, 2, 1).astype(f, copy=False))
    d["w2"] = ac(np.asarray(inputs["L1_ffn_w2"]).astype(f, copy=False))
    d["b2"] = ac(np.asarray(inputs["L1_ffn_b2"]).astype(f, copy=False))
    d["eye"] = np.eye(128, dtype=f)
    d["tri"] = np.triu(np.ones((128, 128), dtype=f))
    d["ones"] = np.ones((128, 1), dtype=f)
    d["onesrow"] = np.ones((1, 128), dtype=f)
    d["b16"] = ac(np.repeat(np.eye(8, dtype=f), 16, axis=1))
    return d


def dc_ap(t, rows, cols, base=0):
    return bass.AP(t.tensor, base, [[cols, 128], [128 * cols, rows // 128], [1, cols]])


class Cnt:
    def __init__(self):
        self.sq = 0
        self.gq = 0
        self.sqa = 0
        self.sqb = 0
        self.sq4 = [0, 0, 0, 0]
        self.yb = 0
        self.gqa = 0
        self.gqb = 0
        self.ts = 0
        self.vs = 0
        self.ss = 0
        self.wc = 0
        self.wv = 0
        self.lw1 = {i: 0 for i in range(16)}
        self.lw2 = {i: 0 for i in range(16)}
        self.lyb = 0
        self.xnt_ts = [0, 0]
        self.lncp = [0, 0]
        self.last_silu = 0
        self.rsv = 0
        self.wsl = [0] * 8


def build_program(upto="full", dbg=()):
    nc = bass.Bass("TRN2", target_bir_lowering=False, debug=False)
    g = lambda n, s: nc.dram_tensor(n, s, F32, kind="ExternalInput").ap()

    xin = g("x", [T, D])
    wqT0, wkT0, wvT0 = g("wqT0", [1024, 64]), g("wkT0", [1024, 64]), g("wvT0", [1024, 64])
    wg0, aall, ball = g("wg0", [1024, 8]), g("aall", [1024, 128]), g("ball", [128, 1024])
    wq1, wk1, wv1 = g("wq1", [1024, 512]), g("wk1", [1024, 512]), g("wv1", [1024, 1024])
    wgk1, wgk2, bgk2t = g("wgk1", [1024, 16]), g("wgk2", [16, 512]), g("bgk2t", [128, 512])
    wg1, gnwt, wo1 = g("wg1", [1024, 1024]), g("gnwt", [128, 1024]), g("wo1", [1024, 1024])
    gwT, gbt, rmswt = g("gwT", [1024, 8]), g("gbt", [128, 8]), g("rmswt", [128, 8])
    w1, b1c = g("w1", [8, 16, 128, 1024]), g("b1c", [8, 128, 16])
    w2, b2 = g("w2", [8, 2048, 1024]), g("b2", [8, 1024])
    eye, tri = g("eye", [128, 128]), g("tri", [128, 128])
    ones, onesrow, b16 = g("ones", [128, 1]), g("onesrow", [1, 128]), g("b16", [8, 128])
    out = nc.dram_tensor("out", [T, D], F32, kind="ExternalOutput").ap()

    c = Cnt()

    from contextlib import ExitStack
    with ExitStack() as _es:
        en = _es.enter_context
        X = en(nc.sbuf_tensor("X", [128, NTC * 1024], F32))
        EYE = en(nc.sbuf_tensor("EYE", [128, 128], F32))
        TRI = en(nc.sbuf_tensor("TRI", [128, 128], F32))
        ONES = en(nc.sbuf_tensor("ONES", [128, 1], F32))
        ONESR = en(nc.sbuf_tensor("ONESR", [1, 128], F32))
        B16S = en(nc.sbuf_tensor("B16", [8, 128], F32))
        XNF = en(nc.sbuf_tensor("XNF", [128, 8 * 512], F32))
        XNFH = en(nc.sbuf_tensor("XNFH", [128, 8 * 512], BF16))
        XNFL = en(nc.sbuf_tensor("XNFL", [128, 8 * 512], BF16))
        XNTM = en(nc.sbuf_tensor("XNTM", [128, 2048], F32))
        SRED = en(nc.sbuf_tensor("SRED", [128, 8], F32))
        MSC = en(nc.sbuf_tensor("MSC", [128, 16], F32))
        WGK1S = en(nc.sbuf_tensor("WGK1S", [128, 128], F32))
        WGK2S = en(nc.sbuf_tensor("WGK2S", [16, 512], F32))
        BGK2TS = en(nc.sbuf_tensor("BGK2T", [128, 512], F32))
        GNWTS = en(nc.sbuf_tensor("GNWT", [128, 1024], F32))
        GWTS = en(nc.sbuf_tensor("GWTS", [128, 64], F32))
        GBTS = en(nc.sbuf_tensor("GBT", [128, 8], F32))
        RMSWTS = en(nc.sbuf_tensor("RMSWT", [128, 8], F32))
        B1S = en(nc.sbuf_tensor("B1S", [128, 128], F32))
        LNSQ = en(nc.sbuf_tensor("LNSQ", [128, 1024], F32))
        B2S = en(nc.sbuf_tensor("B2S", [1, 1024], F32))
        NORMG = en(nc.sbuf_tensor("NORMG", [128, 16], F32))
        EPSV = en(nc.sbuf_tensor("EPSV", [128, 2], F32))
        EPSB = en(nc.sbuf_tensor("EPSB", [128, 1], BF16))
        ps = en(nc.psum_tensor("ps", [128, 4096], F32))
        sq = en(nc.semaphore("sq"))
        gq = en(nc.semaphore("gq"))
        sqab = (en(nc.semaphore("sqa")), en(nc.semaphore("sqb")))
        sq4 = (en(nc.semaphore("sq4a")), en(nc.semaphore("sq4b")), en(nc.semaphore("sq4c")), en(nc.semaphore("sq4d")))
        gqab = (en(nc.semaphore("gqa")), en(nc.semaphore("gqb")))
        wcs = en(nc.semaphore("wcs"))
        wvs = en(nc.semaphore("wvs"))
        ybs = en(nc.semaphore("ybs"))
        tsem = en(nc.semaphore("ts"))
        vsem = en(nc.semaphore("vs"))
        ssem = en(nc.semaphore("ss"))
        dbg_state = {"dbg": set(dbg)}

        def tap(name, src_ap, shape):
            if name not in dbg_state["dbg"]:
                return
            dt_ = nc.dram_tensor("dbg_" + name, list(shape), F32, kind="ExternalOutput").ap()
            with nc.Block() as blk:
                @blk.gpsimd
                def _(e):
                    e.dma_start(out=dt_[:], in_=src_ap).then_inc(gq, 16)
                    c.gq += 16
                    e.wait_ge(gq, c.gq)

        with nc.Block() as blk:
            @blk.gpsimd
            def _(e):
                e.dma_start(out=X.ap().rearrange("p (a b) -> p a b", b=1024),
                            in_=bass.AP(xin.tensor, 0, [[1024, 128], [131072, 16], [1, 1024]])).then_inc(gq, 16)
                c.gq += 16
                for dst, src in ((EYE, eye), (TRI, tri), (ONES, ones), (ONESR, onesrow), (B16S, b16),
                                 (BGK2TS, bgk2t), (GNWTS, gnwt), (GBTS, gbt), (RMSWTS, rmswt)):
                    e.dma_start(out=dst[:], in_=src[:]).then_inc(gq, 16)
                    c.gq += 16
                e.wait_ge(gq, c.gq)

            @blk.vector
            def _(v):
                v.memset(EPSV[:, 0:1], 1.0000001192092896)
                v.drain()
                v.tensor_copy(EPSB[:], EPSV[:, 0:1])
                v.drain()
                v.tensor_tensor(EPSV[:, 1:2], EPSV[:, 0:1], EPSB[:], ALU.subtract)

            @blk.sync
            def _(e):
                e.dma_start(out=WGK1S.ap().rearrange("p (a b) -> p a b", b=16),
                            in_=dc_ap(wgk1, 1024, 16)).then_inc(sq, 16)
                e.dma_start(out=WGK2S[:], in_=wgk2[:]).then_inc(sq, 16)
                e.dma_start(out=GWTS.ap().rearrange("p (a b) -> p a b", b=8),
                            in_=dc_ap(gwT, 1024, 8)).then_inc(sq, 16)
                e.dma_start(out=B1S.ap().rearrange("p (a b) -> p a b", b=16),
                            in_=bass.AP(b1c.tensor, 0, [[16, 128], [2048, 8], [1, 16]])).then_inc(sq, 16)
                c.sq += 4 * 16
                e.wait_ge(sq, c.sq)

        with (nc.sbuf_tensor("B2T", [8, 1024], F32) as B2T, nc.Block() as blk):
            @blk.sync
            def _(e):
                e.dma_start(out=B2T[:], in_=b2[:]).then_inc(sq, 16)
                c.sq += 16

            @blk.tensor
            def _(t):
                t.wait_ge(sq, c.sq)
                t.matmul(ps[0:1, 0:512], ONES[0:8, 0:1], B2T[:, 0:512], start=True, stop=True)
                t.matmul(ps[0:1, 512:1024], ONES[0:8, 0:1], B2T[:, 512:1024], start=True, stop=True).then_inc(tsem, 1)
                c.ts += 1

            @blk.vector
            def _(v):
                v.wait_ge(tsem, c.ts)
                v.tensor_copy(B2S[0:1, 0:512], ps[0:1, 0:512])
                v.tensor_copy(B2S[0:1, 512:1024], ps[0:1, 512:1024])

        def ln_stats(blk, tt):
            @blk.vector
            def _(v, tt=tt):
                v.drain()
                for ch in range(4):
                    v.reduce_sum(SRED[:, ch: ch + 1],
                                 X[:, (tt * 4 + ch) * 1024:(tt * 4 + ch + 1) * 1024], axis=AX.X)
                v.drain()
                v.tensor_scalar(MSC[:, 0:4], SRED[:, 0:4], 1.0 / 1024, None, ALU.mult).then_inc(vsem, 1)
                c.vs += 1

            @blk.scalar
            def _(s, tt=tt):
                s.wait_ge(vsem, c.vs - 1)
                for ch in range(4):
                    s.drain()
                    ii = s.activation(LNSQ[:],
                                      X[:, (tt * 4 + ch) * 1024:(tt * 4 + ch + 1) * 1024],
                                      AF.Square, accum_out=SRED[:, 4 + ch: 5 + ch])
                ii.then_inc(ssem, 1)
                c.ss += 1

            @blk.vector
            def _(v):
                v.wait_ge(ssem, c.ss)
                v.drain()
                v.tensor_scalar(MSC[:, 4:8], SRED[:, 4:8], 1.0 / 1024, None, ALU.mult)
                v.drain()
                v.tensor_tensor(MSC[:, 8:12], MSC[:, 0:4], MSC[:, 0:4], ALU.mult)
                v.drain()
                v.tensor_tensor(MSC[:, 8:12], MSC[:, 4:8], MSC[:, 8:12], ALU.subtract)
                v.drain()
                v.tensor_scalar(MSC[:, 8:12], MSC[:, 8:12], 1e-5, None, ALU.add).then_inc(vsem, 1)
                c.vs += 1

            @blk.scalar
            def _(s):
                s.wait_ge(vsem, c.vs)
                s.activation(MSC[:, 12:16], MSC[:, 8:12], AF.Sqrt).then_inc(ssem, 1)
                c.ss += 1

            @blk.vector
            def _(v):
                v.wait_ge(ssem, c.ss)
                v.reciprocal(MSC[:, 12:16], MSC[:, 12:16]).then_inc(vsem, 1)
                c.vs += 1

        def ln_norm(blk, tt, hilo=False):
            for ch in range(4):
                ic = tt * 4 + ch
                xsl = X[:, ic * 1024:(ic + 1) * 1024]
                xslot = (ch % 2) * 1024
                po = 0 if ch % 2 == 0 else 3072

                @blk.vector
                def _(v, xsl=xsl, ch=ch, xslot=xslot):
                    v.wait_ge(tsem, c.xnt_ts[ch % 2])
                    v.drain()
                    v.tensor_scalar(XNTM[:, xslot: xslot + 1024], xsl, MSC[:, ch: ch + 1],
                                    MSC[:, 12 + ch: 13 + ch], ALU.subtract, ALU.mult).then_inc(vsem, 1)
                    c.vs += 1

                @blk.tensor
                def _(t, po=po, xslot=xslot, ch=ch):
                    t.wait_ge(vsem, c.vs)
                    t.wait_ge(ssem, c.lncp[ch % 2])
                    for dcb in range(8):
                        ii = t.transpose(ps[:, po + dcb * 128: po + dcb * 128 + 128],
                                    XNTM[:, xslot + dcb * 128: xslot + dcb * 128 + 128], EYE[:])
                    ii.then_inc(tsem, 1)
                    c.ts += 1
                c.xnt_ts[ch % 2] = c.ts

                @blk.scalar
                def _(s, ch=ch, po=po, hilo=hilo):
                    s.wait_ge(tsem, c.ts)
                    for dcb in range(8):
                        co = slice(dcb * 512 + ch * 128, dcb * 512 + ch * 128 + 128)
                        pso = ps[:, po + dcb * 128: po + dcb * 128 + 128]
                        ii = s.copy(XNF[:, co], pso)
                        if hilo:
                            ii = s.copy(XNFH[:, co], pso)
                    ii.then_inc(ssem, 1)
                    c.ss += 1
                c.lncp[ch % 2] = c.ss

                if hilo:
                    @blk.vector
                    def _(v, ch=ch, po=po):
                        v.wait_ge(ssem, c.ss)
                        for dcb in range(8):
                            co = slice(dcb * 512 + ch * 128, dcb * 512 + ch * 128 + 128)
                            pso = ps[:, po + dcb * 128: po + dcb * 128 + 128]
                            ii = v.tensor_tensor(XNFL[:, co], pso, XNFH[:, co], ALU.subtract)
                        ii.then_inc(vsem, 1)
                        c.vs += 1

        if upto in ("full", "l0", "l0proj", "l1attn"):
            with ExitStack() as _es0:
                en0 = _es0.enter_context
                WQT0 = en0(nc.sbuf_tensor("WQT0", [128, 512], F32))
                WKT0 = en0(nc.sbuf_tensor("WKT0", [128, 512], F32))
                WVT0 = en0(nc.sbuf_tensor("WVT0", [128, 512], F32))
                WG0S = en0(nc.sbuf_tensor("WG0S", [128, 64], F32))
                AALLS = en0(nc.sbuf_tensor("AALL", [128, 1024], F32))
                BALLS = en0(nc.sbuf_tensor("BALL", [128, 1024], F32))
                QF = en0(nc.sbuf_tensor("QF", [64, 2048], F32))
                KF = en0(nc.sbuf_tensor("KF", [64, 2048], F32))
                VT = en0(nc.sbuf_tensor("VT", [128, 1024], F32))
                HME = en0(nc.sbuf_tensor("HME", [128, 512], F32))
                CWF = en0(nc.sbuf_tensor("CWF", [8, 512], F32))
                CWB = en0(nc.sbuf_tensor("CWB", [128, 512], F32))
                STRIP = en0(nc.sbuf_tensor("STRIP", [128, 2 * 2048], F32))
                VTIL = en0(nc.sbuf_tensor("VTIL", [128, 2 * 64], F32))
                T2A = en0(nc.sbuf_tensor("T2A", [128, 32], F32))
                T2B = en0(nc.sbuf_tensor("T2B", [128, 32], F32))
                T2C = en0(nc.sbuf_tensor("T2C", [128, 32], F32))
                T2D = en0(nc.sbuf_tensor("T2D", [128, 32], F32))
                T2S = en0(nc.sbuf_tensor("T2S", [128, 24], F32))
                ZROW = en0(nc.sbuf_tensor("ZROW", [128, 2], F32))
                _l0cm = nc.Block()
                blk = _l0cm.__enter__()

                @blk.sync
                def _(e):
                    for dst, src, w in ((WQT0, wqT0, 64), (WKT0, wkT0, 64), (WVT0, wvT0, 64),
                                        (WG0S, wg0, 8)):
                        e.dma_start(out=dst.ap().rearrange("p (a b) -> p a b", b=w),
                                    in_=dc_ap(src, 1024, w)).then_inc(sq, 16)
                        c.sq += 16
                    e.dma_start(out=AALLS.ap().rearrange("p (a b) -> p a b", b=128),
                                in_=dc_ap(aall, 1024, 128)).then_inc(sq, 16)
                    c.sq += 16
                    e.dma_start(out=BALLS[:], in_=ball[:]).then_inc(sq, 16)
                    c.sq += 16
                ln_stats(blk, 0)
                for tt in range(4):
                    ln_norm(blk, tt)

                    @blk.tensor
                    def _(t):
                        t.wait_ge(sq, c.sq)
                        t.wait_ge(vsem, c.vs)
                        t.wait_ge(ssem, c.ss)
                        for bnk, wsb in ((0, WQT0), (1, WKT0)):
                            for dc in range(8):
                                t.matmul(ps[0:64, 512 * bnk: 512 * bnk + 512],
                                         wsb[:, dc * 64:(dc + 1) * 64],
                                         XNF[:, dc * 512:(dc + 1) * 512],
                                         start=(dc == 0), stop=(dc == 7))
                        for ch in range(4):
                            for dc in range(8):
                                ii = t.matmul(ps[:, 1024 + ch * 64: 1024 + ch * 64 + 64],
                                         XNF[:, dc * 512 + ch * 128: dc * 512 + ch * 128 + 128],
                                         WVT0[:, dc * 64:(dc + 1) * 64],
                                         start=(ch == 0 and dc == 0), stop=(ch == 3 and dc == 7))
                        ii.then_inc(tsem, 1)
                        c.ts += 1

                    @blk.vector
                    def _(v, tt=tt):
                        v.wait_ge(tsem, c.ts)
                        v.tensor_copy(QF[:, tt * 512:(tt + 1) * 512], ps[0:64, 0:512])
                        v.tensor_copy(KF[:, tt * 512:(tt + 1) * 512], ps[0:64, 512:1024])
                        v.tensor_copy(VT[:, tt * 256:(tt + 1) * 256], ps[:, 1024:1280]).then_inc(vsem, 1)
                        c.vs += 1

                    def t2(q):
                        return T2S[:, q * 4: q * 4 + 4]

                    @blk.tensor
                    def _(t):
                        t.wait_ge(vsem, c.vs)
                        t.wait_ge(ssem, c.ss)
                        for ch in range(4):
                            for dc in range(8):
                                ii = t.matmul(ps[:, 1536 + ch * 8: 1544 + ch * 8],
                                         XNF[:, dc * 512 + ch * 128: dc * 512 + ch * 128 + 128],
                                         WG0S[:, dc * 8:(dc + 1) * 8],
                                         start=(dc == 0), stop=(dc == 7))
                        ii.then_inc(tsem, 1)
                        c.ts += 1

                    @blk.vector
                    def _(v):
                        v.wait_ge(tsem, c.ts)
                        for ch in range(4):
                            pg = ps[:, 1536 + ch * 8: 1544 + ch * 8]
                            v.reduce_max(T2S[:, ch: ch + 1], pg, axis=AX.X)
                            v.reduce_max(T2S[:, 4 + ch: 5 + ch], pg, axis=AX.X, negate=True)
                        v.drain()
                        for ch in range(4):
                            v.tensor_scalar(T2A[:, ch * 8:(ch + 1) * 8],
                                            ps[:, 1536 + ch * 8: 1544 + ch * 8],
                                            T2S[:, ch: ch + 1], None, ALU.is_equal)
                        v.drain()
                        for ch in range(4):
                            v.scalar_tensor_tensor(T2B[:, ch * 8:(ch + 1) * 8],
                                                   T2A[:, ch * 8:(ch + 1) * 8], -1e30,
                                                   ps[:, 1536 + ch * 8: 1544 + ch * 8],
                                                   ALU.mult, ALU.add)
                        v.drain()
                        for ch in range(4):
                            ii = v.reduce_max(T2S[:, 8 + ch: 9 + ch], T2B[:, ch * 8:(ch + 1) * 8],
                                              axis=AX.X)
                        ii.then_inc(vsem, 1)
                        c.vs += 1

                    @blk.scalar
                    def _(s):
                        s.wait_ge(vsem, c.vs)
                        for ch in range(4):
                            s.activation(T2C[:, ch * 8:(ch + 1) * 8],
                                         ps[:, 1536 + ch * 8: 1544 + ch * 8], AF.Exp,
                                         bias=T2S[:, 4 + ch: 5 + ch])
                        s.drain()
                        for ch in range(4):
                            ii = s.activation(T2S[:, 12 + ch: 13 + ch], T2S[:, 8 + ch: 9 + ch],
                                              AF.Exp, bias=T2S[:, 4 + ch: 5 + ch])
                        ii.then_inc(ssem, 1)
                        c.ss += 1

                    @blk.vector
                    def _(v):
                        v.wait_ge(ssem, c.ss)
                        v.tensor_scalar(t2(4), t2(3), 1.0, None, ALU.add)
                        v.drain()
                        v.reciprocal(t2(5), t2(4))
                        v.drain()
                        for ch in range(4):
                            v.tensor_scalar(T2D[:, ch * 8:(ch + 1) * 8],
                                            ps[:, 1536 + ch * 8: 1544 + ch * 8],
                                            T2S[:, 8 + ch: 9 + ch], None, ALU.is_ge)
                        v.drain()
                        for ch in range(4):
                            v.tensor_tensor(T2D[:, ch * 8:(ch + 1) * 8], T2D[:, ch * 8:(ch + 1) * 8],
                                            T2C[:, ch * 8:(ch + 1) * 8], ALU.mult)
                        v.drain()
                        for ch in range(4):
                            ii = v.tensor_scalar(T2D[:, ch * 8:(ch + 1) * 8], T2D[:, ch * 8:(ch + 1) * 8],
                                                 T2S[:, 20 + ch: 21 + ch], None, ALU.mult)
                        ii.then_inc(vsem, 1)
                        c.vs += 1

                    @blk.tensor
                    def _(t):
                        t.wait_ge(vsem, c.vs)
                        for ch in range(4):
                            ii = t.transpose(ps[0:8, 2048 + ch * 128: 2048 + ch * 128 + 128],
                                             T2D[:, ch * 8:(ch + 1) * 8], EYE[:])
                        ii.then_inc(tsem, 1)
                        c.ts += 1

                    @blk.vector
                    def _(v):
                        v.wait_ge(tsem, c.ts)
                        for ch in range(4):
                            ii = v.tensor_copy(CWF[:, ch * 128:(ch + 1) * 128],
                                          ps[0:8, 2048 + ch * 128: 2048 + ch * 128 + 128])
                        ii.then_inc(vsem, 1)
                        c.vs += 1
                    cwf_done = c.vs

                    if tt < 3:
                        ln_stats(blk, tt + 1)

                    @blk.tensor
                    def _(t, cwf_done=cwf_done):
                        t.wait_ge(vsem, cwf_done)
                        t.matmul(ps[:, 2560:3072], B16S[:], CWF[:], start=True, stop=True)
                        for dc in range(8):
                            ii = t.matmul(ps[:, 3072:3584],
                                     AALLS[:, dc * 128:(dc + 1) * 128],
                                     XNF[:, dc * 512:(dc + 1) * 512],
                                     start=(dc == 0), stop=(dc == 7))
                        ii.then_inc(tsem, 1)
                        c.ts += 1

                    @blk.vector
                    def _(v):
                        v.wait_ge(tsem, c.ts)
                        v.tensor_copy(CWB[:], ps[:, 2560:3072])
                        v.drain()
                        v.tensor_tensor(HME[:], ps[:, 3072:3584], CWB[:], ALU.mult).then_inc(vsem, 1)
                        c.vs += 1

                    for ch in range(4):
                        for dh in range(2):
                            yb = 7 if (ch * 2 + dh) % 2 == 0 else 0

                            @blk.tensor
                            def _(t, ch=ch, dh=dh, yb=yb):
                                t.wait_ge(vsem, c.vs)
                                t.matmul(ps[:, yb * 512: yb * 512 + 512],
                                         HME[:, ch * 128:(ch + 1) * 128],
                                         BALLS[:, dh * 512:(dh + 1) * 512],
                                         start=True, stop=True).then_inc(tsem, 1)
                                c.ts += 1

                            @blk.vector
                            def _(v, tt=tt, ch=ch, dh=dh, yb=yb):
                                v.wait_ge(tsem, c.ts)
                                v.drain()
                                xs = X[:, (tt * 4 + ch) * 1024 + dh * 512: (tt * 4 + ch) * 1024 + dh * 512 + 512]
                                v.tensor_tensor(xs, xs, ps[:, yb * 512: yb * 512 + 512], ALU.add).then_inc(vsem, 1)
                                c.vs += 1


                if upto in ("full", "l0", "l1attn"):
                    if True:
                        sc_ts, av_ts, vt_vs = [], [], []

                        def emit_av(kc):
                            @blk.tensor
                            def _(t, kc=kc):
                                t.wait_ge(vsem, vt_vs[kc])
                                sl = (kc % 2) * 2048
                                for ic in range(16):
                                    ii = t.matmul(ps[:, 2048 + ic * 64: 2048 + ic * 64 + 64],
                                             STRIP[:, sl + ic * 128: sl + ic * 128 + 128],
                                             VTIL[:, (kc % 2) * 64:(kc % 2) * 64 + 64],
                                             start=(kc == 0 and ic % 8 == 0),
                                             stop=(kc == 15 and ic % 8 == 7))
                                ii.then_inc(tsem, 1)
                                c.ts += 1
                            av_ts.append(c.ts)

                        for kc in range(16):
                            @blk.tensor
                            def _(t, kc=kc):
                                t.wait_ge(ssem, c.ss)
                                if kc == 0:
                                    t.wait_ge(vsem, c.vs)
                                for it in range(4):
                                    ii = t.matmul(ps[:, it * 512:(it + 1) * 512],
                                             KF[:, kc * 128:(kc + 1) * 128],
                                             QF[:, it * 512:(it + 1) * 512],
                                             start=True, stop=True)
                                ii.then_inc(tsem, 1)
                                c.ts += 1
                            sc_ts.append(c.ts)

                            if kc >= 1:
                                emit_av(kc - 1)

                            @blk.vector
                            def _(v, kc=kc):
                                v.wait_ge(tsem, sc_ts[kc])
                                v.reduce_max(ZROW[:, 0:1], ps[:, 0:2048], axis=AX.X, negate=True).then_inc(vsem, 1)
                                c.vs += 1

                            @blk.scalar
                            def _(s, kc=kc):
                                s.wait_ge(vsem, c.vs)
                                if kc >= 2:
                                    s.wait_ge(tsem, av_ts[kc - 2])
                                sl = (kc % 2) * 2048
                                s.activation(STRIP[:, sl: sl + 2048], ps[:, 0:2048], AF.Exp,
                                             bias=ZROW[:, 0:1],
                                             accum_out=ZROW[:, 1:2]).then_inc(ssem, 1)
                                c.ss += 1

                            @blk.vector
                            def _(v, kc=kc):
                                v.wait_ge(ssem, c.ss)
                                v.reciprocal(ZROW[:, 1:2], ZROW[:, 1:2])
                                v.drain()
                                v.tensor_scalar(VTIL[:, (kc % 2) * 64:(kc % 2) * 64 + 64],
                                                VT[:, kc * 64:(kc + 1) * 64],
                                                ZROW[:, 1:2], None, ALU.mult).then_inc(vsem, 1)
                                c.vs += 1
                            vt_vs.append(c.vs)

                        emit_av(15)

                        @blk.vector
                        def _(v):
                            v.wait_ge(tsem, c.ts)
                            for ic in range(16):
                                v.drain()
                                bb = bass.AP(ps, 2048 + ic * 64, [[4096, 128], [0, 16], [1, 64]])
                                xs = X[:, ic * 1024:(ic + 1) * 1024]
                                ii = v.tensor_tensor(xs, xs, bb, ALU.add)
                            ii.then_inc(vsem, 1)
                            c.vs += 1

                if upto in ("full", "l1attn"):
                    ln_stats(blk, 0)

                _l0cm.__exit__(None, None, None)
                tap("xnf_t3", XNF[:], (128, 4096))
                tap("qf", QF[:], (64, 2048))
                tap("kf", KF[:], (64, 2048))
                tap("vt", VT[:], (128, 1024))
                tap("cwf_t3", CWF[:], (8, 512))
                tap("cwb_t3", CWB[:], (128, 512))
                tap("hme_t3", HME[:], (128, 512))
                tap("x_l0proj", X[:], (128, NTC * 1024))
                tap("x_l0", X[:], (128, NTC * 1024))

        if upto in ("full", "l1attn"):
            with ExitStack() as _es1:
                en1 = _es1.enter_context
                HS = en1(nc.sbuf_tensor("HS", [128, 256], F32))
                with nc.Block() as blk:
                    @blk.vector
                    def _(v):
                        v.memset(HS[:], 0.0)
                        v.drain()
                        v.memset(NORMG[:], 0.0)

                common = dict(X=X, XNF=XNF, ps=ps, EYE=EYE, TRI=TRI, ONES=ONES, ONESR=ONESR,
                              sq4=sq4, ybs=ybs,
                              WGK1S=WGK1S, WGK2S=WGK2S, BGK2T=BGK2TS, GNWT=GNWTS, MSC=MSC,
                              XNFH=XNFH, XNFL=XNFL, B1S=B1S, B2S=B2S, NORMG=NORMG, HS=HS,
                              wq1=wq1, wk1=wk1, wv1=wv1, wg1=wg1, wo1=wo1,
                              w1=w1, w2=w2, wcs=wcs, wvs=wvs)

                for tt in range(4):
                  with ExitStack() as _esa:
                    ena = lambda n_, s_, d_=F32: _esa.enter_context(nc.sbuf_tensor(f"{n_}_{tt}", s_, d_))
                    bufs = dict(common)
                    bufs.update(
                        WSLAB=ena("WSLAB", [128, 4 * 1024]),
                        QT=ena("QT", [128, 1024]), KT=ena("KT", [128, 1024]),
                        VTT=ena("VTT", [128, 2048]), SPT=ena("SPT", [128, 1024]),
                        GT=ena("GT", [128, 2048]), OG=ena("OG", [128, 2048]),
                        OGF=ena("OGF", [128, 2048]), EXN=ena("EXN", [128, 1024]),
                        SQS=ena("SQS", [128, 1024]),
                        EB=ena("EB", [128, 512]), ENB=ena("ENB", [128, 512]),
                        QTL=ena("QTL", [128, 512]), KTL=ena("KTL", [128, 512]),
                        QFT=ena("QFT", [128, 512]), KFT=ena("KFT", [128, 512]),
                        ATS=ena("ATS", [128, 2 * 128]), DECS=ena("DECS", [128, 4]),
                        RSTT=ena("RSTT", [128, 16]), Z1S=ena("Z1S", [128, 32]),
                        Z1F=ena("Z1F", [16, 256]), GSC=ena("GSC", [128, 32]))
                    GSC = bufs["GSC"]
                    QT, KT, VTT, SPT, GT, OG = (bufs[k] for k in ("QT", "KT", "VTT", "SPT", "GT", "OG"))
                    with nc.Block() as blk:
                        ln_norm(blk, tt, hilo=True)

                        @blk.tensor
                        def _(t):
                            t.wait_ge(ssem, c.ss)
                            t.wait_ge(vsem, c.vs)
                            for ch in range(4):
                                for dc in range(8):
                                    ii = t.matmul(ps[:, 1536 + ch * 8: 1544 + ch * 8],
                                             XNF[:, dc * 512 + ch * 128: dc * 512 + ch * 128 + 128],
                                             GWTS[:, dc * 8:(dc + 1) * 8],
                                             start=(dc == 0), stop=(dc == 7))
                            ii.then_inc(tsem, 1)
                            c.ts += 1

                        @blk.vector
                        def _(v):
                            v.wait_ge(tsem, c.ts)
                            for ch in range(4):
                                ii = v.tensor_tensor(GSC[:, ch * 8:(ch + 1) * 8],
                                                     ps[:, 1536 + ch * 8: 1544 + ch * 8],
                                                     GBTS[:], ALU.add)
                            ii.then_inc(vsem, 1)
                            c.vs += 1

                        @blk.scalar
                        def _(s):
                            s.wait_ge(vsem, c.vs)
                            for ch in range(4):
                                ii = s.activation(ps[:, 1536 + ch * 8: 1544 + ch * 8],
                                                  GSC[:, ch * 8:(ch + 1) * 8], AF.Square,
                                                  accum_out=MSC[:, ch: ch + 1])
                            ii.then_inc(ssem, 1)
                            c.ss += 1

                        @blk.vector
                        def _(v):
                            v.wait_ge(ssem, c.ss)
                            v.drain()
                            v.tensor_scalar(MSC[:, 4:8], MSC[:, 0:4], 1.0 / 8, EPSV[:, 1:2],
                                            ALU.mult, ALU.add).then_inc(vsem, 1)
                            c.vs += 1

                        @blk.scalar
                        def _(s):
                            s.wait_ge(vsem, c.vs)
                            s.activation(MSC[:, 4:8], MSC[:, 4:8], AF.Sqrt).then_inc(ssem, 1)
                            c.ss += 1

                        @blk.vector
                        def _(v):
                            v.wait_ge(ssem, c.ss)
                            v.reciprocal(MSC[:, 4:8], MSC[:, 4:8])
                            v.drain()
                            for ch in range(4):
                                v.tensor_scalar(GSC[:, ch * 8:(ch + 1) * 8], GSC[:, ch * 8:(ch + 1) * 8],
                                                MSC[:, 4 + ch: 5 + ch], None, ALU.mult)
                            v.drain()
                            for ch in range(4):
                                v.tensor_tensor(GSC[:, ch * 8:(ch + 1) * 8], GSC[:, ch * 8:(ch + 1) * 8],
                                                RMSWTS[:], ALU.mult)
                            v.drain()
                            for ch in range(4):
                                v.reduce_max(MSC[:, 8 + ch: 9 + ch], GSC[:, ch * 8:(ch + 1) * 8],
                                             axis=AX.X, negate=True)
                            v.drain()
                            v.tensor_scalar(MSC[:, 8:12], MSC[:, 8:12], 2.0, None,
                                            ALU.mult).then_inc(vsem, 1)
                            c.vs += 1

                        @blk.scalar
                        def _(s):
                            s.wait_ge(vsem, c.vs)
                            for ch in range(4):
                                ii = s.activation(GSC[:, ch * 8:(ch + 1) * 8], GSC[:, ch * 8:(ch + 1) * 8],
                                                  AF.Exp, scale=2.0, bias=MSC[:, 8 + ch: 9 + ch],
                                                  accum_out=MSC[:, 12 + ch: 13 + ch])
                            ii.then_inc(ssem, 1)
                            c.ss += 1

                        @blk.vector
                        def _(v, tt=tt):
                            v.wait_ge(ssem, c.ss)
                            for ch in range(4):
                                v.reduce_max(MSC[:, 8 + ch: 9 + ch], GSC[:, ch * 8:(ch + 1) * 8],
                                             axis=AX.X)
                            v.reciprocal(MSC[:, 12:16], MSC[:, 12:16])
                            v.drain()
                            v.tensor_tensor(MSC[:, 8:12], MSC[:, 8:12], MSC[:, 12:16], ALU.mult)
                            v.drain()
                            v.tensor_scalar(MSC[:, 12:16], MSC[:, 8:12], 1e-6, None, ALU.add)
                            v.drain()
                            v.reciprocal(MSC[:, 12:16], MSC[:, 12:16])
                            v.drain()
                            v.tensor_tensor(NORMG[:, tt * 4: tt * 4 + 4], MSC[:, 8:12], MSC[:, 12:16],
                                            ALU.mult).then_inc(vsem, 1)
                            c.vs += 1

                        for hf in range(2):
                            l1_attn_half(nc, blk, c, tt, hf, bufs, sqab, gqab, tsem, vsem, ssem)
                            if hf == 0 and tt < 3:
                                ln_stats(blk, tt + 1)

                  if upto == "full":
                    with ExitStack() as _esf:
                        enf = lambda n_, s_, d_: _esf.enter_context(nc.sbuf_tensor(f"{n_}_{tt}", s_, d_))
                        fbufs = dict(common)
                        fbufs.update(
                            W1SF=enf("W1SF", [128, 2 * 1024], F32),
                            W2SF=enf("W2SF", [128, 2 * 1024], F32),
                            W1SH=enf("W1SH", [128, 2 * 1024], BF16),
                            W1SL=enf("W1SL", [128, 2 * 1024], BF16),
                            W2SH=enf("W2SH", [128, 2 * 1024], BF16),
                            W2SL=enf("W2SL", [128, 2 * 1024], BF16),
                            H1FH=enf("H1FH", [128, 16 * 512], BF16),
                            H1FL=enf("H1FL", [128, 16 * 512], BF16),
                            HTMP=enf("HTMP", [128, 2 * 512], F32))
                        with nc.Block() as blk2:
                            l1_ffn_tt(nc, blk2, c, tt, fbufs, sqab, gqab, tsem, vsem, ssem)

                  if tt == 0:
                        tap("qt_h1", QT[:], (128, 1024))
                        tap("kt_h1", KT[:], (128, 1024))
                        tap("vtt_h1", VTT[:], (128, 2048))
                        tap("spt_h1", SPT[:], (128, 1024))
                        tap("hs_t0", HS[:], (128, 256))
                        tap("og_h1", OG[:], (128, 2048))
                        tap("gt_h1", GT[:], (128, 2048))
                        tap("normg", NORMG[:], (128, 16))
                        tap("x_t0", X[:], (128, NTC * 1024))

        with nc.Block() as blk:
            @blk.gpsimd
            def _(e):
                e.dma_start(out=bass.AP(out.tensor, 0, [[1024, 128], [131072, 16], [1, 1024]]),
                            in_=X.ap().rearrange("p (a b) -> p a b", b=1024)).then_inc(gq, 16)
                c.gq += 16
                e.wait_ge(gq, c.gq)

    return nc




def l1_attn_half(nc, blk, c, tt, hf, B, sqab, gqab, tsem, vsem, ssem):
    sq4 = B["sq4"]
    """GLA for tokens [tt*512 + hf*256, +256) (2 chunks); adds (o_gated @ Wo) into X."""
    X, XNF, ps = B["X"], B["XNF"], B["ps"]
    base_ch = hf * 2
    gch0 = tt * 4 + base_ch
    half_v0 = c.vs

    def proj2(wt, wcols, evac, bank0, extra_v, extra_s):
        nslab = wcols // 512
        nslots = min(4, 4096 // wcols)
        dma_waits = []
        mm_ts0 = c.ts
        mm_ts = [mm_ts0 + dc + 1 for dc in range(8)]
        wsl0 = list(c.wsl)

        @blk.sync
        def _(e):
            for dc in range(8):
                slot = dc % nslots
                if dc >= nslots:
                    e.wait_ge(tsem, mm_ts[dc - nslots])
                else:
                    prev = max(wsl0[i] for i in range(slot * nslab, (slot + 1) * nslab))
                    if prev:
                        e.wait_ge(tsem, prev)
                sem = sq4[dc % 4]
                if c.sq4[dc % 4]:
                    e.wait_ge(sem, c.sq4[dc % 4])
                e.dma_start(out=B["WSLAB"][:, slot * wcols: slot * wcols + wcols],
                            in_=bass.AP(wt.tensor, dc * 128 * wcols,
                                        [[wcols, 128], [1, wcols]])).then_inc(sem, 16)
                c.sq4[dc % 4] += 16
                dma_waits.append((sem, c.sq4[dc % 4]))
        for i in range(nslots * nslab):
            c.wsl[i] = mm_ts[i // nslab + 4]

        @blk.tensor
        def _(t):
            for dc in range(8):
                slot = dc % nslots
                t.wait_ge(*dma_waits[dc])
                if dc == 0:
                    for w in extra_v:
                        t.wait_ge(vsem, w)
                    for w in extra_s:
                        t.wait_ge(ssem, w)
                for ch in range(2):
                    for dh in range(nslab):
                        ii = t.matmul(ps[:, (bank0 + ch * nslab + dh) * 512:
                                          (bank0 + ch * nslab + dh) * 512 + 512],
                                 XNF[:, dc * 512 + (base_ch + ch) * 128:
                                     dc * 512 + (base_ch + ch) * 128 + 128],
                                 B["WSLAB"][:, slot * wcols + dh * 512:
                                            slot * wcols + dh * 512 + 512],
                                 start=(dc == 0), stop=(dc == 7))
                ii.then_inc(tsem, 1)
                c.ts += 1
                assert c.ts == mm_ts[dc]

        evac(2 * nslab, bank0)
        return mm_ts

    def mk_evac(dst):
        def evac(nb, bank0):
            @blk.vector
            def _(v):
                v.wait_ge(tsem, c.ts)
                for b in range(nb):
                    ii = v.tensor_copy(dst[:, b * 512:(b + 1) * 512],
                                       ps[:, (bank0 + b) * 512:(bank0 + b + 1) * 512])
                ii.then_inc(vsem, 1)
            c.vs += 1
        return evac

    half_s0 = c.ss
    q_mm = proj2(B["wq1"], 512, mk_evac(B["QT"]), 0, [half_v0], [half_s0])
    q_ev = c.vs
    k_mm = proj2(B["wk1"], 512, mk_evac(B["KT"]), 2, [half_v0], [half_s0])
    k_ev = c.vs
    v_mm = proj2(B["wv1"], 1024, mk_evac(B["VTT"]), 4, [half_v0], [half_s0])
    v_ev = c.vs

    def evac_g(nb, bank0):
        @blk.scalar
        def _(s):
            s.wait_ge(tsem, c.ts)
            for b in range(nb):
                ii = s.activation(B["GT"][:, b * 512:(b + 1) * 512],
                                  ps[:, (bank0 + b) * 512:(bank0 + b + 1) * 512], AF.Silu)
            ii.then_inc(ssem, 1)
        c.ss += 1
        c.last_silu = c.ss

        @blk.vector
        def _(v):
            v.wait_ge(ssem, c.ss)
            for chh in range(2):
                for dh in range(2):
                    sl = B["GT"][:, chh * 1024 + dh * 512: chh * 1024 + dh * 512 + 512]
                    ii = v.tensor_tensor(sl, sl, B["GNWT"][:, dh * 512:(dh + 1) * 512], ALU.mult)
            ii.then_inc(vsem, 1)
        c.vs += 1

    g_mm = proj2(B["wg1"], 1024, evac_g, 0, [q_ev, k_ev], [c.last_silu])

    @blk.tensor
    def _(t):
        t.wait_ge(vsem, v_ev)
        for ch in range(2):
            for dc in range(8):
                ii = t.matmul(ps[:, 2048 + ch * 16: 2048 + ch * 16 + 16],
                         XNF[:, dc * 512 + (base_ch + ch) * 128: dc * 512 + (base_ch + ch) * 128 + 128],
                         B["WGK1S"][:, dc * 16:(dc + 1) * 16],
                         start=(ch == 0 and dc == 0), stop=(ch == 1 and dc == 7))
        ii.then_inc(tsem, 1)
        c.ts += 1

    @blk.vector
    def _(v):
        v.wait_ge(tsem, c.ts)
        v.tensor_copy(B["Z1S"][:], ps[:, 2048:2080]).then_inc(vsem, 1)
        c.vs += 1

    @blk.tensor
    def _(t):
        t.wait_ge(vsem, c.vs)
        for ch in range(2):
            ii = t.transpose(ps[0:16, 2560 + ch * 128: 2560 + ch * 128 + 128],
                        B["Z1S"][:, ch * 16:(ch + 1) * 16], B["EYE"][:])
        ii.then_inc(tsem, 1)
        c.ts += 1

    @blk.vector
    def _(v):
        v.wait_ge(tsem, c.ts)
        v.tensor_copy(B["Z1F"][:], ps[0:16, 2560:2816]).then_inc(vsem, 1)
        c.vs += 1

    @blk.tensor
    def _(t):
        t.wait_ge(vsem, c.vs)
        for ch in range(2):
            ii = t.matmul(ps[:, ch * 512:(ch + 1) * 512],
                     B["Z1F"][:, ch * 128:(ch + 1) * 128], B["WGK2S"][:],
                     start=True, stop=True)
        ii.then_inc(tsem, 1)
        c.ts += 1

    @blk.vector
    def _(v):
        v.wait_ge(tsem, c.ts)
        v.drain()
        for ch in range(2):
            ii = v.tensor_tensor(B["EXN"][:, ch * 512:(ch + 1) * 512],
                                 ps[:, ch * 512:(ch + 1) * 512], B["BGK2T"][:],
                                 ALU.add)
        ii.then_inc(vsem, 1)
        c.vs += 1

    @blk.scalar
    def _(s):
        s.wait_ge(vsem, c.vs)
        for ch in range(2):
            s.activation(B["EXN"][:, ch * 512:(ch + 1) * 512],
                         B["EXN"][:, ch * 512:(ch + 1) * 512], AF.Exp, scale=-1.0)
        s.drain()
        for ch in range(2):
            ii = s.activation(B["SPT"][:, ch * 512:(ch + 1) * 512],
                              B["EXN"][:, ch * 512:(ch + 1) * 512], AF.Ln, bias=1.0)
        ii.then_inc(ssem, 1)
        c.ss += 1

    og_vals = [0, 0]
    o_done = [0, 0]

    def front(ch):
        @blk.tensor
        def _(t, ch=ch):
            t.wait_ge(ssem, c.ss)
            t.wait_ge(vsem, c.vs)
            t.matmul(ps[:, 2560:3072], B["TRI"][:], B["SPT"][:, ch * 512:(ch + 1) * 512],
                     start=True, stop=True).then_inc(tsem, 1)
            c.ts += 1

        @blk.scalar
        def _(s):
            s.wait_ge(tsem, c.ts)
            s.activation(B["EB"][:], ps[:, 2560:3072], AF.Exp, scale=-1.0 / 16)
            s.drain()
            s.activation(B["ENB"][:], ps[:, 2560:3072], AF.Exp, scale=1.0 / 16).then_inc(ssem, 1)
            c.ss += 1

        @blk.vector
        def _(v, ch=ch):
            v.wait_ge(ssem, c.ss)
            v.scalar_tensor_tensor(B["QTL"][:], B["QT"][:, ch * 512:(ch + 1) * 512], RSCALE,
                                   B["EB"][:], ALU.mult, ALU.mult)
            v.drain()
            v.tensor_tensor(B["KTL"][:], B["KT"][:, ch * 512:(ch + 1) * 512], B["ENB"][:],
                            ALU.mult).then_inc(vsem, 1)
            c.vs += 1

        @blk.tensor
        def _(t):
            t.wait_ge(vsem, c.vs)
            for j in range(4):
                t.transpose(ps[:, 3072 + j * 128: 3072 + j * 128 + 128],
                            B["QTL"][:, j * 128:(j + 1) * 128], B["EYE"][:])
            for j in range(4):
                ii = t.transpose(ps[:, 3584 + j * 128: 3584 + j * 128 + 128],
                            B["KTL"][:, j * 128:(j + 1) * 128], B["EYE"][:])
            ii.then_inc(tsem, 1)
            c.ts += 1

        @blk.vector
        def _(v):
            v.wait_ge(tsem, c.ts)
            v.tensor_copy(B["QFT"][:], ps[:, 3072:3584])
            v.tensor_copy(B["KFT"][:], ps[:, 3584:4096]).then_inc(vsem, 1)
            c.vs += 1
        return c.vs

    def hdloop(ch, qft_val):
        ats_vals = []
        for hd in range(16):
            m, j = hd % 4, hd // 4
            aoff = 1024 + (hd % 8) * 128

            @blk.tensor
            def _(t, hd=hd, m=m, j=j, aoff=aoff):
                if hd >= 8:
                    t.wait_ge(vsem, ats_vals[hd - 8])
                else:
                    t.wait_ge(vsem, qft_val)
                t.matmul(ps[:, aoff: aoff + 128],
                         B["KFT"][32 * m: 32 * m + 32, j * 128:(j + 1) * 128],
                         B["QFT"][32 * m: 32 * m + 32, j * 128:(j + 1) * 128],
                         start=True, stop=True,
                         tile_position=(32 * m, 0)).then_inc(tsem, 1)
                c.ts += 1

            @blk.vector
            def _(v, hd=hd, aoff=aoff):
                v.wait_ge(tsem, c.ts)
                v.drain()
                v.tensor_tensor(B["ATS"][:, (hd % 2) * 128: (hd % 2) * 128 + 128],
                                ps[:, aoff: aoff + 128], B["TRI"][:],
                                ALU.mult).then_inc(vsem, 1)
                c.vs += 1
            ats_vals.append(c.vs)

            @blk.tensor
            def _(t, hd=hd, m=m, j=j, ch=ch):
                t.wait_ge(vsem, c.vs)
                t.matmul(ps[:, 64 * hd: 64 * hd + 64],
                         B["QFT"][32 * m: 32 * m + 32, j * 128:(j + 1) * 128],
                         B["HS"][32 * m: 32 * m + 32, j * 64:(j + 1) * 64],
                         start=(hd % 8 == 0), stop=False,
                         tile_position=(32 * m, 0))
                t.matmul(ps[:, 64 * hd: 64 * hd + 64],
                         B["ATS"][:, (hd % 2) * 128: (hd % 2) * 128 + 128],
                         B["VTT"][:, ch * 1024 + 64 * hd: ch * 1024 + 64 * hd + 64],
                         start=False, stop=(hd % 8 == 7))
                t.matmul(ps[32 * m: 32 * m + 32, 2048 + 64 * j: 2048 + 64 * j + 64],
                         B["KTL"][:, 32 * hd: 32 * hd + 32],
                         B["VTT"][:, ch * 1024 + 64 * hd: ch * 1024 + 64 * hd + 64],
                         start=True, stop=True,
                         tile_position=(0, 32 * m)).then_inc(tsem, 1)
                c.ts += 1
        o_done[ch] = c.ts

    def tail(ch):
        @blk.tensor
        def _(t, ch=ch):
            t.wait_ge(ssem, c.ss)
            for g4 in range(4):
                ii = t.matmul(ps[:, 2560 + g4: 2560 + g4 + 1],
                         B["SPT"][:, ch * 512 + g4 * 128: ch * 512 + (g4 + 1) * 128],
                         B["ONES"][:],
                         start=(g4 == 0), stop=(g4 == 3))
            ii.then_inc(tsem, 1)
            c.ts += 1

        @blk.scalar
        def _(s):
            s.wait_ge(tsem, c.ts)
            s.activation(B["DECS"][:], ps[:, 2560:2564], AF.Exp, scale=-1.0 / 16).then_inc(ssem, 1)
            c.ss += 1

        @blk.vector
        def _(v):
            v.wait_ge(ssem, c.ss)
            v.wait_ge(tsem, c.ts)
            for g4 in range(4):
                v.drain()
                v.tensor_tensor(B["HS"][:, g4 * 64:(g4 + 1) * 64], B["HS"][:, g4 * 64:(g4 + 1) * 64],
                                ps[:, 2048 + 64 * g4: 2048 + 64 * g4 + 64], ALU.add)
                v.drain()
                ii = v.tensor_scalar(B["HS"][:, g4 * 64:(g4 + 1) * 64], B["HS"][:, g4 * 64:(g4 + 1) * 64],
                                B["DECS"][:, g4:g4 + 1], None, ALU.mult)
            ii.then_inc(vsem, 1)
            c.vs += 1

    def epi(ch):
        @blk.scalar
        def _(s, ch=ch):
            s.wait_ge(tsem, o_done[ch])
            s.wait_ge(vsem, c.rsv)
            s.activation(B["SQS"][:], ps[:, 0:1024], AF.Square).then_inc(ssem, 1)
            c.ss += 1

        @blk.vector
        def _(v):
            v.wait_ge(ssem, c.ss)
            v.tensor_reduce(B["RSTT"][:], bass.AP(B["SQS"], 0, [[1024, 128], [64, 16], [1, 64]]),
                            AX.X, ALU.add)
            v.drain()
            v.tensor_scalar(B["RSTT"][:], B["RSTT"][:], 1.0 / 64, 1e-5, ALU.mult, ALU.add).then_inc(vsem, 1)
            c.vs += 1
        c.rsv = c.vs

        @blk.scalar
        def _(s):
            s.wait_ge(vsem, c.vs)
            s.activation(B["RSTT"][:], B["RSTT"][:], AF.Sqrt).then_inc(ssem, 1)
            c.ss += 1

        @blk.vector
        def _(v, ch=ch):
            v.wait_ge(ssem, c.ss)
            v.reciprocal(B["RSTT"][:], B["RSTT"][:])
            v.drain()
            for hd in range(16):
                ii = v.scalar_tensor_tensor(B["OG"][:, ch * 1024 + 64 * hd: ch * 1024 + 64 * hd + 64],
                                            ps[:, 64 * hd: 64 * hd + 64],
                                            B["RSTT"][:, hd: hd + 1],
                                            B["GT"][:, ch * 1024 + 64 * hd: ch * 1024 + 64 * hd + 64],
                                            ALU.mult, ALU.mult)
            ii.then_inc(vsem, 1)
            c.vs += 1
        og_vals[ch] = c.vs

    def ogt(rnd):
        @blk.tensor
        def _(t, rnd=rnd):
            t.wait_ge(vsem, og_vals[rnd])
            for k in range(8):
                ii = t.transpose(ps[:, 3072 + k * 128: 3072 + k * 128 + 128],
                            B["OG"][:, rnd * 1024 + k * 128: rnd * 1024 + k * 128 + 128], B["EYE"][:])
            ii.then_inc(tsem, 1)
            c.ts += 1

        @blk.vector
        def _(v, rnd=rnd):
            v.wait_ge(tsem, c.ts)
            v.tensor_copy(B["OGF"][:, rnd * 1024:(rnd + 1) * 1024], ps[:, 3072:4096]).then_inc(vsem, 1)
            c.vs += 1

    for ch in range(2):
        qv = front(ch)
        if ch == 1:
            epi(0)
        hdloop(ch, qv)
        if ch == 1:
            ogt(0)
        tail(ch)
    epi(1)
    ogt(1)

    wo_waits = []
    wo_ts0 = c.ts
    wo_mm = [wo_ts0 + vb + 1 for vb in range(8)]

    wsl0 = list(c.wsl)

    @blk.sync
    def _(e):
        for vb in range(8):
            slot = vb % 4
            e.wait_ge(tsem, max(wsl0[2 * slot], wsl0[2 * slot + 1]) if vb < 4
                      else wo_mm[vb - 4])
            sem = sq4[vb % 4]
            if c.sq4[vb % 4]:
                e.wait_ge(sem, c.sq4[vb % 4])
            e.dma_start(out=B["WSLAB"][:, slot * 1024: slot * 1024 + 1024],
                        in_=bass.AP(B["wo1"].tensor, vb * 128 * 1024,
                                    [[1024, 128], [1, 1024]])).then_inc(sem, 16)
            c.sq4[vb % 4] += 16
            wo_waits.append((sem, c.sq4[vb % 4]))
    for i in range(8):
        c.wsl[i] = wo_mm[i // 2 + 4]

    @blk.tensor
    def _(t):
        t.wait_ge(vsem, c.vs)
        for vb in range(8):
            slot = vb % 4
            t.wait_ge(*wo_waits[vb])
            for ch in range(2):
                for dh in range(2):
                    ii = t.matmul(ps[:, (ch * 2 + dh) * 512: (ch * 2 + dh) * 512 + 512],
                             B["OGF"][:, (ch * 8 + vb) * 128: (ch * 8 + vb) * 128 + 128],
                             B["WSLAB"][:, slot * 1024 + dh * 512: slot * 1024 + dh * 512 + 512],
                             start=(vb == 0), stop=(vb == 7))
            ii.then_inc(tsem, 1)
            c.ts += 1
            assert c.ts == wo_mm[vb]

    @blk.vector
    def _(v):
        v.wait_ge(tsem, c.ts)
        for ch in range(2):
            for dh in range(2):
                v.drain()
                xs = X[:, (gch0 + ch) * 1024 + dh * 512: (gch0 + ch) * 1024 + dh * 512 + 512]
                ii = v.tensor_tensor(xs, xs, ps[:, (ch * 2 + dh) * 512: (ch * 2 + dh) * 512 + 512],
                                     ALU.add)
        ii.then_inc(vsem, 1)
        c.vs += 1




def l1_ffn_tt(nc, blk, c, tt, B, sqab, gqab, tsem, vsem, ssem):
    X, ps = B["X"], B["ps"]
    XH, XL = B["XNFH"], B["XNFL"]
    wcs, wvs = B["wcs"], B["wvs"]
    ybs = B["ybs"]
    pend = None

    def w1split(hc, dma_wait, ts_wait, val):
        @blk.scalar
        def _(s, hc=hc):
            s.wait_ge(*dma_wait)
            s.wait_ge(*ts_wait)
            half = (hc % 2) * 1024
            s.copy(B["W1SH"][:, half: half + 1024],
                   B["W1SF"][:, half: half + 1024]).then_inc(wcs, 1)
            c.wc += 1

        @blk.vector
        def _(v, hc=hc):
            v.wait_ge(wcs, c.wc)
            half = (hc % 2) * 1024
            v.tensor_tensor(B["W1SL"][:, half: half + 1024],
                            B["W1SF"][:, half: half + 1024],
                            B["W1SH"][:, half: half + 1024],
                            ALU.subtract).then_inc(wvs, 1)
            c.wv += 1
            assert c.wv == val, (c.wv, val)

    def w2split(hc, dma_wait, rd_wait, val):
        @blk.scalar
        def _(s, hc=hc):
            s.wait_ge(*dma_wait)
            s.wait_ge(*rd_wait)
            half = (hc % 2) * 1024
            s.copy(B["W2SH"][:, half: half + 1024],
                   B["W2SF"][:, half: half + 1024]).then_inc(wcs, 1)
            c.wc += 1

        @blk.vector
        def _(v, hc=hc):
            v.wait_ge(wcs, c.wc)
            half = (hc % 2) * 1024
            v.tensor_tensor(B["W2SL"][:, half: half + 1024],
                            B["W2SF"][:, half: half + 1024],
                            B["W2SH"][:, half: half + 1024],
                            ALU.subtract).then_inc(wvs, 1)
            c.wv += 1
            assert c.wv == val, (c.wv, val)

    for e in range(8):
        @blk.tensor
        def _pre(t, e=e):
            t.wait_ge(ssem, c.ss)
            if e == 0:
                t.wait_ge(vsem, c.vs)
            t.nop().then_inc(tsem, 1)
            c.ts += 1
        a_ts0 = c.ts
        b_ts0 = a_ts0 + 16

        W0 = c.wv
        w1v, w2v = {}, {}
        if pend is None:
            w1v[0], w1v[1] = W0 + 1, W0 + 2
            i = W0 + 2
        else:
            w1v[0], w1v[1] = W0 - 1, W0
            assert (w1v[0], w1v[1]) == tuple(pend["vals"]), (w1v, pend)
            i = W0
        for g in range(16):
            if g + 2 < 16:
                i += 1
                w1v[g + 2] = i
            if g == 12:
                i += 1
                w2v[0] = i
            if g == 13:
                i += 1
                w2v[1] = i
        for g in range(14):
            i += 1
            w2v[g + 2] = i

        w1_dma, w2_dma = [], []

        @blk.sync
        def _(e2, e=e, w1v=w1v, w2v=w2v):
            for hc in range(16):
                e2.wait_ge(wvs, w1v[hc - 2] if hc >= 2 else c.lw1[14 + hc])
                sem = sqab[hc % 2]
                half = (hc % 2) * 1024
                e2.dma_start(out=B["W1SF"][:, half: half + 1024],
                             in_=bass.AP(B["w1"].tensor, (e * 16 + hc) * 131072,
                                         [[1024, 128], [1, 1024]])).then_inc(sem, 16)
                if hc % 2 == 0:
                    c.sqa += 16
                    w1_dma.append((sem, c.sqa))
                else:
                    c.sqb += 16
                    w1_dma.append((sem, c.sqb))
            for hc in range(16):
                e2.wait_ge(wvs, w2v[hc - 2] if hc >= 2 else c.lw2[14 + hc])
                sem = gqab[hc % 2]
                half = (hc % 2) * 1024
                e2.dma_start(out=B["W2SF"][:, half: half + 1024],
                             in_=bass.AP(B["w2"].tensor, (e * 16 + hc) * 131072,
                                         [[1024, 128], [1, 1024]])).then_inc(sem, 16)
                if hc % 2 == 0:
                    c.gqa += 16
                    w2_dma.append((sem, c.gqa))
                else:
                    c.gqb += 16
                    w2_dma.append((sem, c.gqb))

        if pend is not None:
            assert w1_dma[0] == pend["dma"][0] and w1_dma[1] == pend["dma"][1], \
                (w1_dma[:2], pend)

        ss0 = c.ss
        vs0 = c.vs
        if pend is None:
            w1split(0, w1_dma[0], (tsem, a_ts0), w1v[0])
            w1split(1, w1_dma[1], (tsem, a_ts0), w1v[1])
        pend = None
        for hc in range(16):
            @blk.tensor
            def _(t, hc=hc, e=e, w1v=w1v):
                t.wait_ge(wvs, w1v[hc])
                if hc == 0 and e > 0:
                    t.wait_ge(vsem, vs0 - 1)
                if hc >= 2:
                    t.wait_ge(ssem, ss0 + hc - 1)
                ab = (4 + hc % 2) * 512
                half = (hc % 2) * 1024
                for dc in range(8):
                    t.matmul(ps[:, ab: ab + 512], B["W1SH"][:, half + dc * 128: half + dc * 128 + 128],
                             XH[:, dc * 512:(dc + 1) * 512], start=(dc == 0), stop=False)
                    t.matmul(ps[:, ab: ab + 512], B["W1SH"][:, half + dc * 128: half + dc * 128 + 128],
                             XL[:, dc * 512:(dc + 1) * 512], start=False, stop=False)
                    ii = t.matmul(ps[:, ab: ab + 512], B["W1SL"][:, half + dc * 128: half + dc * 128 + 128],
                                  XH[:, dc * 512:(dc + 1) * 512], start=False, stop=(dc == 7))
                ii.then_inc(tsem, 1)
                c.ts += 1

            @blk.scalar
            def _(s, hc=hc, e=e):
                s.wait_ge(tsem, c.ts)
                if hc >= 2:
                    s.wait_ge(vsem, vs0 + hc - 1)
                ab = (4 + hc % 2) * 512
                s.activation(B["HTMP"][:, (hc % 2) * 512: (hc % 2) * 512 + 512],
                             ps[:, ab: ab + 512], AF.Gelu,
                             bias=B["B1S"][:, e * 16 + hc: e * 16 + hc + 1]).then_inc(ssem, 1)
                c.ss += 1

            @blk.vector
            def _(v, hc=hc):
                v.wait_ge(ssem, c.ss)
                ht = B["HTMP"][:, (hc % 2) * 512: (hc % 2) * 512 + 512]
                v.tensor_copy(B["H1FH"][:, hc * 512:(hc + 1) * 512], ht)
                v.drain()
                v.tensor_tensor(B["H1FL"][:, hc * 512:(hc + 1) * 512], ht,
                                B["H1FH"][:, hc * 512:(hc + 1) * 512],
                                ALU.subtract).then_inc(vsem, 1)
                c.vs += 1

            if hc + 2 < 16:
                w1split(hc + 2, w1_dma[hc + 2], (tsem, a_ts0 + hc + 1), w1v[hc + 2])
            if hc == 12:
                w2split(0, w2_dma[0], (ybs, c.lyb), w2v[0])
            if hc == 13:
                w2split(1, w2_dma[1], (ybs, c.lyb), w2v[1])

        for hc in range(16):
            @blk.tensor
            def _(t, hc=hc, e=e, w2v=w2v):
                t.wait_ge(wvs, w2v[hc])
                t.wait_ge(vsem, vs0 + hc + 1)
                if hc == 0:
                    t.wait_ge(ssem, ss0 + 16)
                half = (hc % 2) * 1024
                for tc in (2, 3, 0, 1):
                    for dh in range(2):
                        yb = (tc * 2 + dh) * 512
                        hsl = slice(hc * 512 + tc * 128, hc * 512 + tc * 128 + 128)
                        t.matmul(ps[:, yb: yb + 512], B["H1FH"][:, hsl],
                                 B["W2SH"][:, half + dh * 512: half + dh * 512 + 512],
                                 start=(hc == 0), stop=False)
                        t.matmul(ps[:, yb: yb + 512], B["H1FH"][:, hsl],
                                 B["W2SL"][:, half + dh * 512: half + dh * 512 + 512],
                                 start=False, stop=False)
                        if e == 7 and hc == 15:
                            t.matmul(ps[:, yb: yb + 512], B["ONESR"][:],
                                     B["B2S"][0:1, dh * 512:(dh + 1) * 512],
                                     start=False, stop=False)
                        ii = t.matmul(ps[:, yb: yb + 512], B["H1FL"][:, hsl],
                                      B["W2SH"][:, half + dh * 512: half + dh * 512 + 512],
                                      start=False, stop=(hc == 15))
                        if hc == 15 and tc == 2 and dh == 1:
                            ii.then_inc(ybs, 1)
                            c.yb += 1
                if hc == 15:
                    ii.then_inc(ybs, 1)
                    c.yb += 1
                else:
                    ii.then_inc(tsem, 1)
                    c.ts += 1

            if hc + 2 < 16:
                w2split(hc + 2, w2_dma[hc + 2], (tsem, b_ts0 + hc + 1), w2v[hc + 2])
            if e < 7 and hc in (14, 15):
                hcn = hc - 14
                dw = (sqab[hcn], (c.sqa if hcn == 0 else c.sqb) + 16)
                w1split(hcn, dw, (tsem, a_ts0 + 15 + hcn), c.wv + 1)
                if hcn == 0:
                    pend = {"dma": [dw], "vals": [c.wv]}
                else:
                    pend["dma"].append(dw)
                    pend["vals"].append(c.wv)

        c.lw1 = w1v
        c.lw2 = w2v
        c.lyb = c.yb

        @blk.vector
        def _(v, tt=tt):
            v.wait_ge(ybs, c.yb - 1)
            for tc2, dh in ((2, 0), (2, 1)):
                v.drain()
                tc = tt * 4 + tc2
                xs = X[:, tc * 1024 + dh * 512: tc * 1024 + dh * 512 + 512]
                ii = v.scalar_tensor_tensor(xs, ps[:, (tc2 * 2 + dh) * 512: (tc2 * 2 + dh) * 512 + 512],
                                            B["NORMG"][:, tc: tc + 1], xs, ALU.mult, ALU.add)
            ii.then_inc(vsem, 1)
            c.vs += 1
            v.wait_ge(ybs, c.yb)
            for tc2, dh in ((3, 0), (3, 1), (0, 0), (0, 1), (1, 0), (1, 1)):
                v.drain()
                tc = tt * 4 + tc2
                xs = X[:, tc * 1024 + dh * 512: tc * 1024 + dh * 512 + 512]
                ii = v.scalar_tensor_tensor(xs, ps[:, (tc2 * 2 + dh) * 512: (tc2 * 2 + dh) * 512 + 512],
                                            B["NORMG"][:, tc: tc + 1], xs, ALU.mult, ALU.add)
            ii.then_inc(vsem, 1)
            c.vs += 1




def kernel(**inputs):
    nc = build_program(upto="full")
    in_maps = [prep_core_inputs(inputs, cc) for cc in range(NCORE)]
    res = run_bass_kernel_spmd(nc, in_maps, core_ids=list(range(NCORE)))
    return np.stack([res.results[cc]["out"] for cc in range(NCORE)], axis=0)


# version 17 (final)
# speedup vs baseline: 1.0272x; 1.0272x over previous
import numpy as np
import concourse.bass as bass
import concourse.mybir as mybir
from concourse.bass_utils import run_bass_kernel_spmd

F32 = mybir.dt.float32
BF16 = mybir.dt.bfloat16
AF = mybir.ActivationFunctionType
ALU = mybir.AluOpType
AX = mybir.AxisListType

NCORE = 8
T = 2048
D = 1024
NTC = 16
F32EPS = float(np.finfo(np.float32).eps)
RSCALE = float(32.0 ** -0.5)



def prep_core_inputs(inputs, c):
    f = np.float32
    ac = np.ascontiguousarray
    d = {}
    d["x"] = ac(np.asarray(inputs["L0_attn_x"])[c].astype(f, copy=False))
    d["wqT0"] = ac(np.asarray(inputs["L0_attn_Wq"]).T.astype(f, copy=False))
    d["wkT0"] = ac(np.asarray(inputs["L0_attn_Wk"]).T.astype(f, copy=False))
    d["wvT0"] = ac(np.asarray(inputs["L0_attn_Wv"]).T.astype(f, copy=False))
    d["wg0"] = ac(np.asarray(inputs["L0_ffn_Wg"]).astype(f, copy=False))
    d["aall"] = ac(np.asarray(inputs["L0_ffn_A"]).transpose(1, 0, 2).reshape(1024, 128).astype(f, copy=False))
    d["ball"] = ac(np.asarray(inputs["L0_ffn_B"]).reshape(128, 1024).astype(f, copy=False))
    d["wq1"] = ac(np.asarray(inputs["L1_attn_Wq"]).astype(f, copy=False))
    d["wk1"] = ac(np.asarray(inputs["L1_attn_Wk"]).astype(f, copy=False))
    d["wv1"] = ac(np.asarray(inputs["L1_attn_Wv"]).astype(f, copy=False))
    d["wgk1"] = ac(np.asarray(inputs["L1_attn_Wgk1"]).astype(f, copy=False))
    d["wgk2"] = ac(np.asarray(inputs["L1_attn_Wgk2"]).astype(f, copy=False))
    d["bgk2t"] = ac(np.broadcast_to(np.asarray(inputs["L1_attn_bgk2"]).astype(f, copy=False), (128, 512)))
    d["wg1"] = ac(np.asarray(inputs["L1_attn_Wg"]).astype(f, copy=False))
    d["gnwt"] = ac(np.broadcast_to(np.tile(np.asarray(inputs["L1_attn_gnw"]).astype(f, copy=False), 16), (128, 1024)))
    d["wo1"] = ac(np.asarray(inputs["L1_attn_Wo"]).astype(f, copy=False))
    d["gwT"] = ac(np.asarray(inputs["L1_ffn_gate_w"]).T.astype(f, copy=False))
    d["gbt"] = ac(np.broadcast_to(np.asarray(inputs["L1_ffn_gate_b"]).astype(f, copy=False), (128, 8)))
    d["rmswt"] = ac(np.broadcast_to(np.asarray(inputs["L1_ffn_rms_w"]).astype(f, copy=False), (128, 8)))
    d["w1"] = ac(np.asarray(inputs["L1_ffn_w1"]).astype(f, copy=False)
                 .reshape(8, 8, 128, 16, 128).transpose(0, 3, 2, 1, 4).reshape(8, 16, 128, 1024))
    d["b1c"] = ac(np.asarray(inputs["L1_ffn_b1"]).reshape(8, 16, 128).transpose(0, 2, 1).astype(f, copy=False))
    d["w2"] = ac(np.asarray(inputs["L1_ffn_w2"]).astype(f, copy=False))
    d["b2"] = ac(np.asarray(inputs["L1_ffn_b2"]).astype(f, copy=False))
    d["eye"] = np.eye(128, dtype=f)
    d["tri"] = np.triu(np.ones((128, 128), dtype=f))
    d["ones"] = np.ones((128, 1), dtype=f)
    d["onesrow"] = np.ones((1, 128), dtype=f)
    d["b16"] = ac(np.repeat(np.eye(8, dtype=f), 16, axis=1))
    return d


def dc_ap(t, rows, cols, base=0):
    return bass.AP(t.tensor, base, [[cols, 128], [128 * cols, rows // 128], [1, cols]])


class Cnt:
    def __init__(self):
        self.sq = 0
        self.gq = 0
        self.sqa = 0
        self.sqb = 0
        self.sq4 = [0, 0, 0, 0]
        self.yb = 0
        self.gqa = 0
        self.gqb = 0
        self.ts = 0
        self.vs = 0
        self.ss = 0
        self.wc = 0
        self.wv = 0
        self.lw1 = {i: 0 for i in range(16)}
        self.lw2 = {i: 0 for i in range(16)}
        self.lyb = 0
        self.xnt_ts = [0, 0]
        self.lncp = [0, 0]
        self.last_silu = 0
        self.rsv = 0
        self.decs_last = 0
        self.wsl = [0] * 8


def build_program(upto="full", dbg=()):
    nc = bass.Bass("TRN2", target_bir_lowering=False, debug=False)
    g = lambda n, s: nc.dram_tensor(n, s, F32, kind="ExternalInput").ap()

    xin = g("x", [T, D])
    wqT0, wkT0, wvT0 = g("wqT0", [1024, 64]), g("wkT0", [1024, 64]), g("wvT0", [1024, 64])
    wg0, aall, ball = g("wg0", [1024, 8]), g("aall", [1024, 128]), g("ball", [128, 1024])
    wq1, wk1, wv1 = g("wq1", [1024, 512]), g("wk1", [1024, 512]), g("wv1", [1024, 1024])
    wgk1, wgk2, bgk2t = g("wgk1", [1024, 16]), g("wgk2", [16, 512]), g("bgk2t", [128, 512])
    wg1, gnwt, wo1 = g("wg1", [1024, 1024]), g("gnwt", [128, 1024]), g("wo1", [1024, 1024])
    gwT, gbt, rmswt = g("gwT", [1024, 8]), g("gbt", [128, 8]), g("rmswt", [128, 8])
    w1, b1c = g("w1", [8, 16, 128, 1024]), g("b1c", [8, 128, 16])
    w2, b2 = g("w2", [8, 2048, 1024]), g("b2", [8, 1024])
    eye, tri = g("eye", [128, 128]), g("tri", [128, 128])
    ones, onesrow, b16 = g("ones", [128, 1]), g("onesrow", [1, 128]), g("b16", [8, 128])
    out = nc.dram_tensor("out", [T, D], F32, kind="ExternalOutput").ap()

    c = Cnt()

    from contextlib import ExitStack
    with ExitStack() as _es:
        en = _es.enter_context
        X = en(nc.sbuf_tensor("X", [128, NTC * 1024], F32))
        EYE = en(nc.sbuf_tensor("EYE", [128, 128], F32))
        TRI = en(nc.sbuf_tensor("TRI", [128, 128], F32))
        ONES = en(nc.sbuf_tensor("ONES", [128, 1], F32))
        ONESR = en(nc.sbuf_tensor("ONESR", [1, 128], F32))
        B16S = en(nc.sbuf_tensor("B16", [8, 128], F32))
        XNF = en(nc.sbuf_tensor("XNF", [128, 8 * 512], F32))
        XNFH = en(nc.sbuf_tensor("XNFH", [128, 8 * 512], BF16))
        XNFL = en(nc.sbuf_tensor("XNFL", [128, 8 * 512], BF16))
        XNTM = en(nc.sbuf_tensor("XNTM", [128, 2048], F32))
        SRED = en(nc.sbuf_tensor("SRED", [128, 8], F32))
        MSC = en(nc.sbuf_tensor("MSC", [128, 16], F32))
        WGK1S = en(nc.sbuf_tensor("WGK1S", [128, 128], F32))
        WGK2S = en(nc.sbuf_tensor("WGK2S", [16, 512], F32))
        BGK2TS = en(nc.sbuf_tensor("BGK2T", [128, 512], F32))
        GNWTS = en(nc.sbuf_tensor("GNWT", [128, 1024], F32))
        GWTS = en(nc.sbuf_tensor("GWTS", [128, 64], F32))
        GBTS = en(nc.sbuf_tensor("GBT", [128, 8], F32))
        RMSWTS = en(nc.sbuf_tensor("RMSWT", [128, 8], F32))
        B1S = en(nc.sbuf_tensor("B1S", [128, 128], F32))
        LNSQ = en(nc.sbuf_tensor("LNSQ", [128, 1024], F32))
        B2S = en(nc.sbuf_tensor("B2S", [1, 1024], F32))
        NORMG = en(nc.sbuf_tensor("NORMG", [128, 16], F32))
        EPSV = en(nc.sbuf_tensor("EPSV", [128, 2], F32))
        EPSB = en(nc.sbuf_tensor("EPSB", [128, 1], BF16))
        ps = en(nc.psum_tensor("ps", [128, 4096], F32))
        sq = en(nc.semaphore("sq"))
        gq = en(nc.semaphore("gq"))
        sqab = (en(nc.semaphore("sqa")), en(nc.semaphore("sqb")))
        sq4 = (en(nc.semaphore("sq4a")), en(nc.semaphore("sq4b")), en(nc.semaphore("sq4c")), en(nc.semaphore("sq4d")))
        gqab = (en(nc.semaphore("gqa")), en(nc.semaphore("gqb")))
        wcs = en(nc.semaphore("wcs"))
        wvs = en(nc.semaphore("wvs"))
        ybs = en(nc.semaphore("ybs"))
        tsem = en(nc.semaphore("ts"))
        vsem = en(nc.semaphore("vs"))
        ssem = en(nc.semaphore("ss"))
        dbg_state = {"dbg": set(dbg)}

        def tap(name, src_ap, shape):
            if name not in dbg_state["dbg"]:
                return
            dt_ = nc.dram_tensor("dbg_" + name, list(shape), F32, kind="ExternalOutput").ap()
            with nc.Block() as blk:
                @blk.gpsimd
                def _(e):
                    e.dma_start(out=dt_[:], in_=src_ap).then_inc(gq, 16)
                    c.gq += 16
                    e.wait_ge(gq, c.gq)

        with nc.Block() as blk:
            @blk.gpsimd
            def _(e):
                e.dma_start(out=X.ap().rearrange("p (a b) -> p a b", b=1024),
                            in_=bass.AP(xin.tensor, 0, [[1024, 128], [131072, 16], [1, 1024]])).then_inc(gq, 16)
                c.gq += 16
                for dst, src in ((EYE, eye), (TRI, tri), (ONES, ones), (ONESR, onesrow), (B16S, b16),
                                 (BGK2TS, bgk2t), (GNWTS, gnwt), (GBTS, gbt), (RMSWTS, rmswt)):
                    e.dma_start(out=dst[:], in_=src[:]).then_inc(gq, 16)
                    c.gq += 16
                e.wait_ge(gq, c.gq)

            @blk.vector
            def _(v):
                v.memset(EPSV[:, 0:1], 1.0000001192092896)
                v.drain()
                v.tensor_copy(EPSB[:], EPSV[:, 0:1])
                v.drain()
                v.tensor_tensor(EPSV[:, 1:2], EPSV[:, 0:1], EPSB[:], ALU.subtract)

            @blk.sync
            def _(e):
                e.dma_start(out=WGK1S.ap().rearrange("p (a b) -> p a b", b=16),
                            in_=dc_ap(wgk1, 1024, 16)).then_inc(sq, 16)
                e.dma_start(out=WGK2S[:], in_=wgk2[:]).then_inc(sq, 16)
                e.dma_start(out=GWTS.ap().rearrange("p (a b) -> p a b", b=8),
                            in_=dc_ap(gwT, 1024, 8)).then_inc(sq, 16)
                e.dma_start(out=B1S.ap().rearrange("p (a b) -> p a b", b=16),
                            in_=bass.AP(b1c.tensor, 0, [[16, 128], [2048, 8], [1, 16]])).then_inc(sq, 16)
                c.sq += 4 * 16
                e.wait_ge(sq, c.sq)

        with (nc.sbuf_tensor("B2T", [8, 1024], F32) as B2T, nc.Block() as blk):
            @blk.sync
            def _(e):
                e.dma_start(out=B2T[:], in_=b2[:]).then_inc(sq, 16)
                c.sq += 16

            @blk.tensor
            def _(t):
                t.wait_ge(sq, c.sq)
                t.matmul(ps[0:1, 0:512], ONES[0:8, 0:1], B2T[:, 0:512], start=True, stop=True)
                t.matmul(ps[0:1, 512:1024], ONES[0:8, 0:1], B2T[:, 512:1024], start=True, stop=True).then_inc(tsem, 1)
                c.ts += 1

            @blk.vector
            def _(v):
                v.wait_ge(tsem, c.ts)
                v.tensor_copy(B2S[0:1, 0:512], ps[0:1, 0:512])
                v.tensor_copy(B2S[0:1, 512:1024], ps[0:1, 512:1024])

        def ln_stats(blk, tt):
            @blk.vector
            def _(v, tt=tt):
                v.drain()
                for ch in range(4):
                    v.reduce_sum(SRED[:, ch: ch + 1],
                                 X[:, (tt * 4 + ch) * 1024:(tt * 4 + ch + 1) * 1024], axis=AX.X)
                v.drain()
                v.tensor_scalar(MSC[:, 0:4], SRED[:, 0:4], 1.0 / 1024, None, ALU.mult).then_inc(vsem, 1)
                c.vs += 1

            @blk.scalar
            def _(s, tt=tt):
                s.wait_ge(vsem, c.vs - 1)
                for ch in range(4):
                    s.drain()
                    ii = s.activation(LNSQ[:],
                                      X[:, (tt * 4 + ch) * 1024:(tt * 4 + ch + 1) * 1024],
                                      AF.Square, accum_out=SRED[:, 4 + ch: 5 + ch])
                ii.then_inc(ssem, 1)
                c.ss += 1

            @blk.vector
            def _(v):
                v.wait_ge(ssem, c.ss)
                v.drain()
                v.tensor_scalar(MSC[:, 4:8], SRED[:, 4:8], 1.0 / 1024, None, ALU.mult)
                v.drain()
                v.tensor_tensor(MSC[:, 8:12], MSC[:, 0:4], MSC[:, 0:4], ALU.mult)
                v.drain()
                v.tensor_tensor(MSC[:, 8:12], MSC[:, 4:8], MSC[:, 8:12], ALU.subtract)
                v.drain()
                v.tensor_scalar(MSC[:, 8:12], MSC[:, 8:12], 1e-5, None, ALU.add).then_inc(vsem, 1)
                c.vs += 1

            @blk.scalar
            def _(s):
                s.wait_ge(vsem, c.vs)
                s.activation(MSC[:, 12:16], MSC[:, 8:12], AF.Sqrt).then_inc(ssem, 1)
                c.ss += 1

            @blk.vector
            def _(v):
                v.wait_ge(ssem, c.ss)
                v.reciprocal(MSC[:, 12:16], MSC[:, 12:16]).then_inc(vsem, 1)
                c.vs += 1

        def ln_norm(blk, tt, hilo=False):
            def ln_sub(ch, po):
                @blk.vector
                def _(v, ch=ch, po=po):
                    v.wait_ge(ssem, c.lncp[ch % 2])
                    for dcb in range(8):
                        co = slice(dcb * 512 + ch * 128, dcb * 512 + ch * 128 + 128)
                        pso = ps[:, po + dcb * 128: po + dcb * 128 + 128]
                        ii = v.tensor_tensor(XNFL[:, co], pso, XNFH[:, co], ALU.subtract)
                    ii.then_inc(vsem, 1)
                    c.vs += 1

            for ch in range(4):
                ic = tt * 4 + ch
                xsl = X[:, ic * 1024:(ic + 1) * 1024]
                xslot = (ch % 2) * 1024
                po = 0 if ch % 2 == 0 else 3072

                @blk.vector
                def _(v, xsl=xsl, ch=ch, xslot=xslot):
                    v.wait_ge(tsem, c.xnt_ts[ch % 2])
                    v.drain()
                    v.tensor_scalar(XNTM[:, xslot: xslot + 1024], xsl, MSC[:, ch: ch + 1],
                                    MSC[:, 12 + ch: 13 + ch], ALU.subtract, ALU.mult).then_inc(vsem, 1)
                    c.vs += 1

                @blk.tensor
                def _(t, po=po, xslot=xslot, ch=ch):
                    t.wait_ge(vsem, c.vs)
                    t.wait_ge(ssem, c.lncp[ch % 2])
                    for dcb in range(8):
                        ii = t.transpose(ps[:, po + dcb * 128: po + dcb * 128 + 128],
                                    XNTM[:, xslot + dcb * 128: xslot + dcb * 128 + 128], EYE[:])
                    ii.then_inc(tsem, 1)
                    c.ts += 1
                c.xnt_ts[ch % 2] = c.ts

                if hilo and ch > 0:
                    ln_sub(ch - 1, 3072 if (ch - 1) % 2 else 0)

                @blk.scalar
                def _(s, ch=ch, po=po, hilo=hilo):
                    s.wait_ge(tsem, c.ts)
                    for dcb in range(8):
                        co = slice(dcb * 512 + ch * 128, dcb * 512 + ch * 128 + 128)
                        pso = ps[:, po + dcb * 128: po + dcb * 128 + 128]
                        ii = s.copy(XNF[:, co], pso)
                        if hilo:
                            ii = s.copy(XNFH[:, co], pso)
                    ii.then_inc(ssem, 1)
                    c.ss += 1
                c.lncp[ch % 2] = c.ss
            if hilo:
                ln_sub(3, 3072)

        if upto in ("full", "l0", "l0proj", "l1attn"):
            with ExitStack() as _es0:
                en0 = _es0.enter_context
                WQT0 = en0(nc.sbuf_tensor("WQT0", [128, 512], F32))
                WKT0 = en0(nc.sbuf_tensor("WKT0", [128, 512], F32))
                WVT0 = en0(nc.sbuf_tensor("WVT0", [128, 512], F32))
                WG0S = en0(nc.sbuf_tensor("WG0S", [128, 64], F32))
                AALLS = en0(nc.sbuf_tensor("AALL", [128, 1024], F32))
                BALLS = en0(nc.sbuf_tensor("BALL", [128, 1024], F32))
                QF = en0(nc.sbuf_tensor("QF", [64, 2048], F32))
                KF = en0(nc.sbuf_tensor("KF", [64, 2048], F32))
                VT = en0(nc.sbuf_tensor("VT", [128, 1024], F32))
                HME = en0(nc.sbuf_tensor("HME", [128, 512], F32))
                CWF = en0(nc.sbuf_tensor("CWF", [8, 512], F32))
                CWB = en0(nc.sbuf_tensor("CWB", [128, 512], F32))
                STRIP = en0(nc.sbuf_tensor("STRIP", [128, 2 * 2048], F32))
                VTIL = en0(nc.sbuf_tensor("VTIL", [128, 2 * 64], F32))
                T2A = en0(nc.sbuf_tensor("T2A", [128, 32], F32))
                T2B = en0(nc.sbuf_tensor("T2B", [128, 32], F32))
                T2C = en0(nc.sbuf_tensor("T2C", [128, 32], F32))
                T2D = en0(nc.sbuf_tensor("T2D", [128, 32], F32))
                T2S = en0(nc.sbuf_tensor("T2S", [128, 24], F32))
                ZROW = en0(nc.sbuf_tensor("ZROW", [128, 2], F32))
                _l0cm = nc.Block()
                blk = _l0cm.__enter__()

                @blk.sync
                def _(e):
                    for dst, src, w in ((WQT0, wqT0, 64), (WKT0, wkT0, 64), (WVT0, wvT0, 64),
                                        (WG0S, wg0, 8)):
                        e.dma_start(out=dst.ap().rearrange("p (a b) -> p a b", b=w),
                                    in_=dc_ap(src, 1024, w)).then_inc(sq, 16)
                        c.sq += 16
                    e.dma_start(out=AALLS.ap().rearrange("p (a b) -> p a b", b=128),
                                in_=dc_ap(aall, 1024, 128)).then_inc(sq, 16)
                    c.sq += 16
                    e.dma_start(out=BALLS[:], in_=ball[:]).then_inc(sq, 16)
                    c.sq += 16
                ln_stats(blk, 0)
                for tt in range(4):
                    ln_norm(blk, tt)

                    @blk.tensor
                    def _(t):
                        t.wait_ge(sq, c.sq)
                        t.wait_ge(vsem, c.vs)
                        t.wait_ge(ssem, c.ss)
                        for bnk, wsb in ((0, WQT0), (1, WKT0)):
                            for dc in range(8):
                                t.matmul(ps[0:64, 512 * bnk: 512 * bnk + 512],
                                         wsb[:, dc * 64:(dc + 1) * 64],
                                         XNF[:, dc * 512:(dc + 1) * 512],
                                         start=(dc == 0), stop=(dc == 7))
                        for ch in range(4):
                            for dc in range(8):
                                ii = t.matmul(ps[:, 1024 + ch * 64: 1024 + ch * 64 + 64],
                                         XNF[:, dc * 512 + ch * 128: dc * 512 + ch * 128 + 128],
                                         WVT0[:, dc * 64:(dc + 1) * 64],
                                         start=(ch == 0 and dc == 0), stop=(ch == 3 and dc == 7))
                        ii.then_inc(tsem, 1)
                        c.ts += 1

                    @blk.vector
                    def _(v, tt=tt):
                        v.wait_ge(tsem, c.ts)
                        v.tensor_copy(QF[:, tt * 512:(tt + 1) * 512], ps[0:64, 0:512])
                        v.tensor_copy(KF[:, tt * 512:(tt + 1) * 512], ps[0:64, 512:1024])
                        v.tensor_copy(VT[:, tt * 256:(tt + 1) * 256], ps[:, 1024:1280]).then_inc(vsem, 1)
                        c.vs += 1

                    def t2(q):
                        return T2S[:, q * 4: q * 4 + 4]

                    @blk.tensor
                    def _(t):
                        t.wait_ge(vsem, c.vs)
                        t.wait_ge(ssem, c.ss)
                        for ch in range(4):
                            for dc in range(8):
                                ii = t.matmul(ps[:, 1536 + ch * 8: 1544 + ch * 8],
                                         XNF[:, dc * 512 + ch * 128: dc * 512 + ch * 128 + 128],
                                         WG0S[:, dc * 8:(dc + 1) * 8],
                                         start=(dc == 0), stop=(dc == 7))
                        ii.then_inc(tsem, 1)
                        c.ts += 1

                    @blk.vector
                    def _(v):
                        v.wait_ge(tsem, c.ts)
                        for ch in range(4):
                            pg = ps[:, 1536 + ch * 8: 1544 + ch * 8]
                            v.reduce_max(T2S[:, ch: ch + 1], pg, axis=AX.X)
                            v.reduce_max(T2S[:, 4 + ch: 5 + ch], pg, axis=AX.X, negate=True)
                        v.drain()
                        for ch in range(4):
                            v.tensor_scalar(T2A[:, ch * 8:(ch + 1) * 8],
                                            ps[:, 1536 + ch * 8: 1544 + ch * 8],
                                            T2S[:, ch: ch + 1], None, ALU.is_equal)
                        v.drain()
                        for ch in range(4):
                            v.scalar_tensor_tensor(T2B[:, ch * 8:(ch + 1) * 8],
                                                   T2A[:, ch * 8:(ch + 1) * 8], -1e30,
                                                   ps[:, 1536 + ch * 8: 1544 + ch * 8],
                                                   ALU.mult, ALU.add)
                        v.drain()
                        for ch in range(4):
                            ii = v.reduce_max(T2S[:, 8 + ch: 9 + ch], T2B[:, ch * 8:(ch + 1) * 8],
                                              axis=AX.X)
                        ii.then_inc(vsem, 1)
                        c.vs += 1

                    @blk.scalar
                    def _(s):
                        s.wait_ge(vsem, c.vs)
                        for ch in range(4):
                            s.activation(T2C[:, ch * 8:(ch + 1) * 8],
                                         ps[:, 1536 + ch * 8: 1544 + ch * 8], AF.Exp,
                                         bias=T2S[:, 4 + ch: 5 + ch])
                        s.drain()
                        for ch in range(4):
                            ii = s.activation(T2S[:, 12 + ch: 13 + ch], T2S[:, 8 + ch: 9 + ch],
                                              AF.Exp, bias=T2S[:, 4 + ch: 5 + ch])
                        ii.then_inc(ssem, 1)
                        c.ss += 1

                    @blk.vector
                    def _(v):
                        v.wait_ge(ssem, c.ss)
                        v.tensor_scalar(t2(4), t2(3), 1.0, None, ALU.add)
                        v.drain()
                        v.reciprocal(t2(5), t2(4))
                        v.drain()
                        for ch in range(4):
                            v.tensor_scalar(T2D[:, ch * 8:(ch + 1) * 8],
                                            ps[:, 1536 + ch * 8: 1544 + ch * 8],
                                            T2S[:, 8 + ch: 9 + ch], None, ALU.is_ge)
                        v.drain()
                        for ch in range(4):
                            v.tensor_tensor(T2D[:, ch * 8:(ch + 1) * 8], T2D[:, ch * 8:(ch + 1) * 8],
                                            T2C[:, ch * 8:(ch + 1) * 8], ALU.mult)
                        v.drain()
                        for ch in range(4):
                            ii = v.tensor_scalar(T2D[:, ch * 8:(ch + 1) * 8], T2D[:, ch * 8:(ch + 1) * 8],
                                                 T2S[:, 20 + ch: 21 + ch], None, ALU.mult)
                        ii.then_inc(vsem, 1)
                        c.vs += 1

                    @blk.tensor
                    def _(t):
                        t.wait_ge(vsem, c.vs)
                        for ch in range(4):
                            ii = t.transpose(ps[0:8, 2048 + ch * 128: 2048 + ch * 128 + 128],
                                             T2D[:, ch * 8:(ch + 1) * 8], EYE[:])
                        ii.then_inc(tsem, 1)
                        c.ts += 1

                    @blk.vector
                    def _(v):
                        v.wait_ge(tsem, c.ts)
                        for ch in range(4):
                            ii = v.tensor_copy(CWF[:, ch * 128:(ch + 1) * 128],
                                          ps[0:8, 2048 + ch * 128: 2048 + ch * 128 + 128])
                        ii.then_inc(vsem, 1)
                        c.vs += 1
                    cwf_done = c.vs

                    if tt < 3:
                        ln_stats(blk, tt + 1)

                    @blk.tensor
                    def _(t, cwf_done=cwf_done):
                        t.wait_ge(vsem, cwf_done)
                        t.matmul(ps[:, 2560:3072], B16S[:], CWF[:], start=True, stop=True)
                        for dc in range(8):
                            ii = t.matmul(ps[:, 3072:3584],
                                     AALLS[:, dc * 128:(dc + 1) * 128],
                                     XNF[:, dc * 512:(dc + 1) * 512],
                                     start=(dc == 0), stop=(dc == 7))
                        ii.then_inc(tsem, 1)
                        c.ts += 1

                    @blk.vector
                    def _(v):
                        v.wait_ge(tsem, c.ts)
                        v.tensor_copy(CWB[:], ps[:, 2560:3072])
                        v.drain()
                        v.tensor_tensor(HME[:], ps[:, 3072:3584], CWB[:], ALU.mult).then_inc(vsem, 1)
                        c.vs += 1

                    for ch in range(4):
                        for dh in range(2):
                            yb = 7 if (ch * 2 + dh) % 2 == 0 else 0

                            @blk.tensor
                            def _(t, ch=ch, dh=dh, yb=yb):
                                t.wait_ge(vsem, c.vs)
                                t.matmul(ps[:, yb * 512: yb * 512 + 512],
                                         HME[:, ch * 128:(ch + 1) * 128],
                                         BALLS[:, dh * 512:(dh + 1) * 512],
                                         start=True, stop=True).then_inc(tsem, 1)
                                c.ts += 1

                            @blk.vector
                            def _(v, tt=tt, ch=ch, dh=dh, yb=yb):
                                v.wait_ge(tsem, c.ts)
                                v.drain()
                                xs = X[:, (tt * 4 + ch) * 1024 + dh * 512: (tt * 4 + ch) * 1024 + dh * 512 + 512]
                                v.tensor_tensor(xs, xs, ps[:, yb * 512: yb * 512 + 512], ALU.add).then_inc(vsem, 1)
                                c.vs += 1


                if upto in ("full", "l0", "l1attn"):
                    if True:
                        sc_ts, av_ts, vt_vs = [], [], []

                        def emit_av(kc):
                            @blk.tensor
                            def _(t, kc=kc):
                                t.wait_ge(vsem, vt_vs[kc])
                                sl = (kc % 2) * 2048
                                for ic in range(16):
                                    ii = t.matmul(ps[:, 2048 + ic * 64: 2048 + ic * 64 + 64],
                                             STRIP[:, sl + ic * 128: sl + ic * 128 + 128],
                                             VTIL[:, (kc % 2) * 64:(kc % 2) * 64 + 64],
                                             start=(kc == 0 and ic % 8 == 0),
                                             stop=(kc == 15 and ic % 8 == 7))
                                ii.then_inc(tsem, 1)
                                c.ts += 1
                            av_ts.append(c.ts)

                        for kc in range(16):
                            @blk.tensor
                            def _(t, kc=kc):
                                t.wait_ge(ssem, c.ss)
                                if kc == 0:
                                    t.wait_ge(vsem, c.vs)
                                for it in range(4):
                                    ii = t.matmul(ps[:, it * 512:(it + 1) * 512],
                                             KF[:, kc * 128:(kc + 1) * 128],
                                             QF[:, it * 512:(it + 1) * 512],
                                             start=True, stop=True)
                                ii.then_inc(tsem, 1)
                                c.ts += 1
                            sc_ts.append(c.ts)

                            if kc >= 1:
                                emit_av(kc - 1)

                            @blk.vector
                            def _(v, kc=kc):
                                v.wait_ge(tsem, sc_ts[kc])
                                v.reduce_max(ZROW[:, 0:1], ps[:, 0:2048], axis=AX.X, negate=True).then_inc(vsem, 1)
                                c.vs += 1

                            @blk.scalar
                            def _(s, kc=kc):
                                s.wait_ge(vsem, c.vs)
                                if kc >= 2:
                                    s.wait_ge(tsem, av_ts[kc - 2])
                                sl = (kc % 2) * 2048
                                s.activation(STRIP[:, sl: sl + 2048], ps[:, 0:2048], AF.Exp,
                                             bias=ZROW[:, 0:1],
                                             accum_out=ZROW[:, 1:2]).then_inc(ssem, 1)
                                c.ss += 1

                            @blk.vector
                            def _(v, kc=kc):
                                v.wait_ge(ssem, c.ss)
                                v.reciprocal(ZROW[:, 1:2], ZROW[:, 1:2])
                                v.drain()
                                v.tensor_scalar(VTIL[:, (kc % 2) * 64:(kc % 2) * 64 + 64],
                                                VT[:, kc * 64:(kc + 1) * 64],
                                                ZROW[:, 1:2], None, ALU.mult).then_inc(vsem, 1)
                                c.vs += 1
                            vt_vs.append(c.vs)

                        emit_av(15)

                        @blk.vector
                        def _(v):
                            v.wait_ge(tsem, c.ts)
                            for ic in range(16):
                                v.drain()
                                bb = bass.AP(ps, 2048 + ic * 64, [[4096, 128], [0, 16], [1, 64]])
                                xs = X[:, ic * 1024:(ic + 1) * 1024]
                                ii = v.tensor_tensor(xs, xs, bb, ALU.add)
                            ii.then_inc(vsem, 1)
                            c.vs += 1

                if upto in ("full", "l1attn"):
                    ln_stats(blk, 0)

                _l0cm.__exit__(None, None, None)
                tap("xnf_t3", XNF[:], (128, 4096))
                tap("qf", QF[:], (64, 2048))
                tap("kf", KF[:], (64, 2048))
                tap("vt", VT[:], (128, 1024))
                tap("cwf_t3", CWF[:], (8, 512))
                tap("cwb_t3", CWB[:], (128, 512))
                tap("hme_t3", HME[:], (128, 512))
                tap("x_l0proj", X[:], (128, NTC * 1024))
                tap("x_l0", X[:], (128, NTC * 1024))

        if upto in ("full", "l1attn"):
            with ExitStack() as _es1:
                en1 = _es1.enter_context
                HS = en1(nc.sbuf_tensor("HS", [128, 256], F32))
                with nc.Block() as blk:
                    @blk.vector
                    def _(v):
                        v.memset(HS[:], 0.0)
                        v.drain()
                        v.memset(NORMG[:], 0.0)

                common = dict(X=X, XNF=XNF, ps=ps, EYE=EYE, TRI=TRI, ONES=ONES, ONESR=ONESR,
                              sq4=sq4, ybs=ybs,
                              WGK1S=WGK1S, WGK2S=WGK2S, BGK2T=BGK2TS, GNWT=GNWTS, MSC=MSC,
                              XNFH=XNFH, XNFL=XNFL, B1S=B1S, B2S=B2S, NORMG=NORMG, HS=HS,
                              wq1=wq1, wk1=wk1, wv1=wv1, wg1=wg1, wo1=wo1,
                              w1=w1, w2=w2, wcs=wcs, wvs=wvs)

                for tt in range(4):
                  with ExitStack() as _esa:
                    ena = lambda n_, s_, d_=F32: _esa.enter_context(nc.sbuf_tensor(f"{n_}_{tt}", s_, d_))
                    bufs = dict(common)
                    bufs.update(
                        WSLAB=ena("WSLAB", [128, 4 * 1024]),
                        QT=ena("QT", [128, 1024]), KT=ena("KT", [128, 1024]),
                        VTT=ena("VTT", [128, 2048]), SPT=ena("SPT", [128, 1024]),
                        GT=ena("GT", [128, 2048]), OG=ena("OG", [128, 2048]),
                        OGF=ena("OGF", [128, 2048]), EXN=ena("EXN", [128, 1024]),
                        SQS=ena("SQS", [128, 1024]),
                        EB=ena("EB", [128, 512]), ENB=ena("ENB", [128, 512]),
                        QTL=ena("QTL", [128, 512]), KTL=ena("KTL", [128, 512]),
                        QFT=ena("QFT", [128, 512]), KFT=ena("KFT", [128, 512]),
                        ATS=ena("ATS", [128, 2 * 128]), DECS=ena("DECS", [128, 4]),
                        RSTT=ena("RSTT", [128, 16]), Z1S=ena("Z1S", [128, 32]),
                        Z1F=ena("Z1F", [16, 256]), GSC=ena("GSC", [128, 32]))
                    GSC = bufs["GSC"]
                    QT, KT, VTT, SPT, GT, OG = (bufs[k] for k in ("QT", "KT", "VTT", "SPT", "GT", "OG"))
                    with nc.Block() as blk:
                        ln_norm(blk, tt, hilo=True)

                        @blk.tensor
                        def _(t):
                            t.wait_ge(ssem, c.ss)
                            t.wait_ge(vsem, c.vs)
                            for ch in range(4):
                                for dc in range(8):
                                    ii = t.matmul(ps[:, 1536 + ch * 8: 1544 + ch * 8],
                                             XNF[:, dc * 512 + ch * 128: dc * 512 + ch * 128 + 128],
                                             GWTS[:, dc * 8:(dc + 1) * 8],
                                             start=(dc == 0), stop=(dc == 7))
                            ii.then_inc(tsem, 1)
                            c.ts += 1

                        @blk.vector
                        def _(v):
                            v.wait_ge(tsem, c.ts)
                            for ch in range(4):
                                ii = v.tensor_tensor(GSC[:, ch * 8:(ch + 1) * 8],
                                                     ps[:, 1536 + ch * 8: 1544 + ch * 8],
                                                     GBTS[:], ALU.add)
                            ii.then_inc(vsem, 1)
                            c.vs += 1

                        @blk.scalar
                        def _(s):
                            s.wait_ge(vsem, c.vs)
                            for ch in range(4):
                                ii = s.activation(ps[:, 1536 + ch * 8: 1544 + ch * 8],
                                                  GSC[:, ch * 8:(ch + 1) * 8], AF.Square,
                                                  accum_out=MSC[:, ch: ch + 1])
                            ii.then_inc(ssem, 1)
                            c.ss += 1

                        @blk.vector
                        def _(v):
                            v.wait_ge(ssem, c.ss)
                            v.drain()
                            v.tensor_scalar(MSC[:, 4:8], MSC[:, 0:4], 1.0 / 8, EPSV[:, 1:2],
                                            ALU.mult, ALU.add).then_inc(vsem, 1)
                            c.vs += 1

                        @blk.scalar
                        def _(s):
                            s.wait_ge(vsem, c.vs)
                            s.activation(MSC[:, 4:8], MSC[:, 4:8], AF.Sqrt).then_inc(ssem, 1)
                            c.ss += 1

                        @blk.vector
                        def _(v):
                            v.wait_ge(ssem, c.ss)
                            v.reciprocal(MSC[:, 4:8], MSC[:, 4:8])
                            v.drain()
                            for ch in range(4):
                                v.tensor_scalar(GSC[:, ch * 8:(ch + 1) * 8], GSC[:, ch * 8:(ch + 1) * 8],
                                                MSC[:, 4 + ch: 5 + ch], None, ALU.mult)
                            v.drain()
                            for ch in range(4):
                                v.tensor_tensor(GSC[:, ch * 8:(ch + 1) * 8], GSC[:, ch * 8:(ch + 1) * 8],
                                                RMSWTS[:], ALU.mult)
                            v.drain()
                            for ch in range(4):
                                v.reduce_max(MSC[:, 8 + ch: 9 + ch], GSC[:, ch * 8:(ch + 1) * 8],
                                             axis=AX.X, negate=True)
                            v.drain()
                            v.tensor_scalar(MSC[:, 8:12], MSC[:, 8:12], 2.0, None,
                                            ALU.mult).then_inc(vsem, 1)
                            c.vs += 1

                        @blk.scalar
                        def _(s):
                            s.wait_ge(vsem, c.vs)
                            for ch in range(4):
                                ii = s.activation(GSC[:, ch * 8:(ch + 1) * 8], GSC[:, ch * 8:(ch + 1) * 8],
                                                  AF.Exp, scale=2.0, bias=MSC[:, 8 + ch: 9 + ch],
                                                  accum_out=MSC[:, 12 + ch: 13 + ch])
                            ii.then_inc(ssem, 1)
                            c.ss += 1

                        @blk.vector
                        def _(v, tt=tt):
                            v.wait_ge(ssem, c.ss)
                            for ch in range(4):
                                v.reduce_max(MSC[:, 8 + ch: 9 + ch], GSC[:, ch * 8:(ch + 1) * 8],
                                             axis=AX.X)
                            v.reciprocal(MSC[:, 12:16], MSC[:, 12:16])
                            v.drain()
                            v.tensor_tensor(MSC[:, 8:12], MSC[:, 8:12], MSC[:, 12:16], ALU.mult)
                            v.drain()
                            v.tensor_scalar(MSC[:, 12:16], MSC[:, 8:12], 1e-6, None, ALU.add)
                            v.drain()
                            v.reciprocal(MSC[:, 12:16], MSC[:, 12:16])
                            v.drain()
                            v.tensor_tensor(NORMG[:, tt * 4: tt * 4 + 4], MSC[:, 8:12], MSC[:, 12:16],
                                            ALU.mult).then_inc(vsem, 1)
                            c.vs += 1

                        for hf in range(2):
                            l1_attn_half(nc, blk, c, tt, hf, bufs, sqab, gqab, tsem, vsem, ssem)
                            if hf == 0 and tt < 3:
                                ln_stats(blk, tt + 1)

                  if upto == "full":
                    with ExitStack() as _esf:
                        enf = lambda n_, s_, d_: _esf.enter_context(nc.sbuf_tensor(f"{n_}_{tt}", s_, d_))
                        fbufs = dict(common)
                        fbufs.update(
                            W1SF=enf("W1SF", [128, 2 * 1024], F32),
                            W2SF=enf("W2SF", [128, 2 * 1024], F32),
                            W1SH=enf("W1SH", [128, 2 * 1024], BF16),
                            W1SL=enf("W1SL", [128, 2 * 1024], BF16),
                            W2SH=enf("W2SH", [128, 2 * 1024], BF16),
                            W2SL=enf("W2SL", [128, 2 * 1024], BF16),
                            H1FH=enf("H1FH", [128, 16 * 512], BF16),
                            H1FL=enf("H1FL", [128, 16 * 512], BF16),
                            HTMP=enf("HTMP", [128, 2 * 512], F32))
                        with nc.Block() as blk2:
                            l1_ffn_tt(nc, blk2, c, tt, fbufs, sqab, gqab, tsem, vsem, ssem)

                  if tt == 0:
                        tap("qt_h1", QT[:], (128, 1024))
                        tap("kt_h1", KT[:], (128, 1024))
                        tap("vtt_h1", VTT[:], (128, 2048))
                        tap("spt_h1", SPT[:], (128, 1024))
                        tap("hs_t0", HS[:], (128, 256))
                        tap("og_h1", OG[:], (128, 2048))
                        tap("gt_h1", GT[:], (128, 2048))
                        tap("normg", NORMG[:], (128, 16))
                        tap("x_t0", X[:], (128, NTC * 1024))

        with nc.Block() as blk:
            @blk.gpsimd
            def _(e):
                e.dma_start(out=bass.AP(out.tensor, 0, [[1024, 128], [131072, 16], [1, 1024]]),
                            in_=X.ap().rearrange("p (a b) -> p a b", b=1024)).then_inc(gq, 16)
                c.gq += 16
                e.wait_ge(gq, c.gq)

    return nc




def l1_attn_half(nc, blk, c, tt, hf, B, sqab, gqab, tsem, vsem, ssem):
    sq4 = B["sq4"]
    """GLA for tokens [tt*512 + hf*256, +256) (2 chunks); adds (o_gated @ Wo) into X."""
    X, XNF, ps = B["X"], B["XNF"], B["ps"]
    base_ch = hf * 2
    gch0 = tt * 4 + base_ch
    half_v0 = c.vs

    def proj2(wt, wcols, evac, bank0, extra_v, extra_s):
        nslab = wcols // 512
        nslots = min(4, 4096 // wcols)
        dma_waits = []
        mm_ts0 = c.ts
        mm_ts = [mm_ts0 + dc + 1 for dc in range(8)]
        wsl0 = list(c.wsl)

        @blk.sync
        def _(e):
            for dc in range(8):
                slot = dc % nslots
                if dc >= nslots:
                    e.wait_ge(tsem, mm_ts[dc - nslots])
                else:
                    prev = max(wsl0[i] for i in range(slot * nslab, (slot + 1) * nslab))
                    if prev:
                        e.wait_ge(tsem, prev)
                sem = sq4[dc % 4]
                if c.sq4[dc % 4]:
                    e.wait_ge(sem, c.sq4[dc % 4])
                e.dma_start(out=B["WSLAB"][:, slot * wcols: slot * wcols + wcols],
                            in_=bass.AP(wt.tensor, dc * 128 * wcols,
                                        [[wcols, 128], [1, wcols]])).then_inc(sem, 16)
                c.sq4[dc % 4] += 16
                dma_waits.append((sem, c.sq4[dc % 4]))
        for i in range(nslots * nslab):
            c.wsl[i] = mm_ts[i // nslab + 4]

        @blk.tensor
        def _(t):
            for dc in range(8):
                slot = dc % nslots
                t.wait_ge(*dma_waits[dc])
                if dc == 0:
                    for w in extra_v:
                        t.wait_ge(vsem, w)
                    for w in extra_s:
                        t.wait_ge(ssem, w)
                for ch in range(2):
                    for dh in range(nslab):
                        ii = t.matmul(ps[:, (bank0 + ch * nslab + dh) * 512:
                                          (bank0 + ch * nslab + dh) * 512 + 512],
                                 XNF[:, dc * 512 + (base_ch + ch) * 128:
                                     dc * 512 + (base_ch + ch) * 128 + 128],
                                 B["WSLAB"][:, slot * wcols + dh * 512:
                                            slot * wcols + dh * 512 + 512],
                                 start=(dc == 0), stop=(dc == 7))
                ii.then_inc(tsem, 1)
                c.ts += 1
                assert c.ts == mm_ts[dc]

        evac(2 * nslab, bank0)
        return mm_ts

    def mk_evac(dst):
        def evac(nb, bank0):
            @blk.vector
            def _(v):
                v.wait_ge(tsem, c.ts)
                for b in range(nb):
                    ii = v.tensor_copy(dst[:, b * 512:(b + 1) * 512],
                                       ps[:, (bank0 + b) * 512:(bank0 + b + 1) * 512])
                ii.then_inc(vsem, 1)
            c.vs += 1
        return evac

    half_s0 = c.ss
    q_mm = proj2(B["wq1"], 512, mk_evac(B["QT"]), 0, [half_v0], [half_s0])
    q_ev = c.vs
    k_mm = proj2(B["wk1"], 512, mk_evac(B["KT"]), 2, [half_v0], [half_s0])
    k_ev = c.vs
    v_mm = proj2(B["wv1"], 1024, mk_evac(B["VTT"]), 4, [half_v0], [half_s0])
    v_ev = c.vs

    def evac_g(nb, bank0):
        @blk.scalar
        def _(s):
            s.wait_ge(tsem, c.ts)
            for b in range(nb):
                ii = s.activation(B["GT"][:, b * 512:(b + 1) * 512],
                                  ps[:, (bank0 + b) * 512:(bank0 + b + 1) * 512], AF.Silu)
            ii.then_inc(ssem, 1)
        c.ss += 1
        c.last_silu = c.ss

        @blk.vector
        def _(v):
            v.wait_ge(ssem, c.ss)
            for chh in range(2):
                for dh in range(2):
                    sl = B["GT"][:, chh * 1024 + dh * 512: chh * 1024 + dh * 512 + 512]
                    ii = v.tensor_tensor(sl, sl, B["GNWT"][:, dh * 512:(dh + 1) * 512], ALU.mult)
            ii.then_inc(vsem, 1)
        c.vs += 1

    g_mm = proj2(B["wg1"], 1024, evac_g, 0, [q_ev, k_ev], [c.last_silu])

    @blk.tensor
    def _(t):
        t.wait_ge(vsem, v_ev)
        for ch in range(2):
            for dc in range(8):
                ii = t.matmul(ps[:, 2048 + ch * 16: 2048 + ch * 16 + 16],
                         XNF[:, dc * 512 + (base_ch + ch) * 128: dc * 512 + (base_ch + ch) * 128 + 128],
                         B["WGK1S"][:, dc * 16:(dc + 1) * 16],
                         start=(ch == 0 and dc == 0), stop=(ch == 1 and dc == 7))
        ii.then_inc(tsem, 1)
        c.ts += 1

    @blk.vector
    def _(v):
        v.wait_ge(tsem, c.ts)
        v.tensor_copy(B["Z1S"][:], ps[:, 2048:2080]).then_inc(vsem, 1)
        c.vs += 1

    @blk.tensor
    def _(t):
        t.wait_ge(vsem, c.vs)
        for ch in range(2):
            ii = t.transpose(ps[0:16, 2560 + ch * 128: 2560 + ch * 128 + 128],
                        B["Z1S"][:, ch * 16:(ch + 1) * 16], B["EYE"][:])
        ii.then_inc(tsem, 1)
        c.ts += 1

    @blk.vector
    def _(v):
        v.wait_ge(tsem, c.ts)
        v.tensor_copy(B["Z1F"][:], ps[0:16, 2560:2816]).then_inc(vsem, 1)
        c.vs += 1

    @blk.tensor
    def _(t):
        t.wait_ge(vsem, c.vs)
        for ch in range(2):
            ii = t.matmul(ps[:, ch * 512:(ch + 1) * 512],
                     B["Z1F"][:, ch * 128:(ch + 1) * 128], B["WGK2S"][:],
                     start=True, stop=True)
        ii.then_inc(tsem, 1)
        c.ts += 1

    spt_ss = []

    @blk.vector
    def _(v):
        v.wait_ge(tsem, c.ts)
        v.drain()
        for ch in range(2):
            v.tensor_tensor(B["EXN"][:, ch * 512:(ch + 1) * 512],
                            ps[:, ch * 512:(ch + 1) * 512], B["BGK2T"][:],
                            ALU.add).then_inc(vsem, 1)
            c.vs += 1

    for ch in range(2):
        @blk.scalar
        def _(s, ch=ch):
            s.wait_ge(vsem, c.vs - 1 + ch)
            s.activation(B["EXN"][:, ch * 512:(ch + 1) * 512],
                         B["EXN"][:, ch * 512:(ch + 1) * 512], AF.Exp, scale=-1.0)
            s.drain()
            s.activation(B["SPT"][:, ch * 512:(ch + 1) * 512],
                         B["EXN"][:, ch * 512:(ch + 1) * 512], AF.Ln,
                         bias=1.0).then_inc(ssem, 1)
            c.ss += 1
        spt_ss.append(c.ss)

    og_vals = [0, 0]
    o_done = [0, 0]

    def front(ch):
        @blk.tensor
        def _(t, ch=ch):
            t.wait_ge(ssem, max(spt_ss[ch], c.decs_last))
            t.wait_ge(vsem, c.vs)
            t.matmul(ps[:, 2560:3072], B["TRI"][:], B["SPT"][:, ch * 512:(ch + 1) * 512],
                     start=True, stop=True).then_inc(tsem, 1)
            c.ts += 1

        @blk.scalar
        def _(s):
            s.wait_ge(tsem, c.ts)
            s.activation(B["EB"][:], ps[:, 2560:3072], AF.Exp, scale=-1.0 / 16)
            s.activation(B["ENB"][:], ps[:, 2560:3072], AF.Exp, scale=1.0 / 16).then_inc(ssem, 1)
            c.ss += 1

        @blk.vector
        def _(v, ch=ch):
            v.wait_ge(ssem, c.ss)
            v.scalar_tensor_tensor(B["QTL"][:], B["QT"][:, ch * 512:(ch + 1) * 512], RSCALE,
                                   B["EB"][:], ALU.mult, ALU.mult)
            v.tensor_tensor(B["KTL"][:], B["KT"][:, ch * 512:(ch + 1) * 512], B["ENB"][:],
                            ALU.mult).then_inc(vsem, 1)
            c.vs += 1

        @blk.tensor
        def _(t):
            t.wait_ge(vsem, c.vs)
            for j in range(4):
                t.transpose(ps[:, 3072 + j * 128: 3072 + j * 128 + 128],
                            B["QTL"][:, j * 128:(j + 1) * 128], B["EYE"][:])
            for j in range(4):
                ii = t.transpose(ps[:, 3584 + j * 128: 3584 + j * 128 + 128],
                            B["KTL"][:, j * 128:(j + 1) * 128], B["EYE"][:])
            ii.then_inc(tsem, 1)
            c.ts += 1

        @blk.vector
        def _(v):
            v.wait_ge(tsem, c.ts)
            v.tensor_copy(B["QFT"][:], ps[:, 3072:3584])
            v.tensor_copy(B["KFT"][:], ps[:, 3584:4096]).then_inc(vsem, 1)
            c.vs += 1
        return c.vs

    def hdloop(ch, qft_val):
        ats_vals = []
        for hd in range(16):
            m, j = hd % 4, hd // 4
            aoff = 1024 + (hd % 8) * 128

            @blk.tensor
            def _(t, hd=hd, m=m, j=j, aoff=aoff):
                if hd >= 8:
                    t.wait_ge(vsem, ats_vals[hd - 8])
                else:
                    t.wait_ge(vsem, qft_val)
                t.matmul(ps[:, aoff: aoff + 128],
                         B["KFT"][32 * m: 32 * m + 32, j * 128:(j + 1) * 128],
                         B["QFT"][32 * m: 32 * m + 32, j * 128:(j + 1) * 128],
                         start=True, stop=True,
                         tile_position=(32 * m, 0)).then_inc(tsem, 1)
                c.ts += 1

            @blk.vector
            def _(v, hd=hd, aoff=aoff):
                v.wait_ge(tsem, c.ts)
                v.drain()
                v.tensor_tensor(B["ATS"][:, (hd % 2) * 128: (hd % 2) * 128 + 128],
                                ps[:, aoff: aoff + 128], B["TRI"][:],
                                ALU.mult).then_inc(vsem, 1)
                c.vs += 1
            ats_vals.append(c.vs)

            @blk.tensor
            def _(t, hd=hd, m=m, j=j, ch=ch):
                t.wait_ge(vsem, c.vs)
                t.matmul(ps[:, 64 * hd: 64 * hd + 64],
                         B["QFT"][32 * m: 32 * m + 32, j * 128:(j + 1) * 128],
                         B["HS"][32 * m: 32 * m + 32, j * 64:(j + 1) * 64],
                         start=(hd % 8 == 0), stop=False,
                         tile_position=(32 * m, 0))
                t.matmul(ps[:, 64 * hd: 64 * hd + 64],
                         B["ATS"][:, (hd % 2) * 128: (hd % 2) * 128 + 128],
                         B["VTT"][:, ch * 1024 + 64 * hd: ch * 1024 + 64 * hd + 64],
                         start=False, stop=(hd % 8 == 7))
                t.matmul(ps[32 * m: 32 * m + 32, 2048 + 64 * j: 2048 + 64 * j + 64],
                         B["KTL"][:, 32 * hd: 32 * hd + 32],
                         B["VTT"][:, ch * 1024 + 64 * hd: ch * 1024 + 64 * hd + 64],
                         start=True, stop=True,
                         tile_position=(0, 32 * m)).then_inc(tsem, 1)
                c.ts += 1
        o_done[ch] = c.ts

    def tail(ch):
        @blk.tensor
        def _(t, ch=ch):
            t.wait_ge(ssem, c.ss)
            for g4 in range(4):
                ii = t.matmul(ps[:, 2560 + g4: 2560 + g4 + 1],
                         B["SPT"][:, ch * 512 + g4 * 128: ch * 512 + (g4 + 1) * 128],
                         B["ONES"][:],
                         start=(g4 == 0), stop=(g4 == 3))
            ii.then_inc(tsem, 1)
            c.ts += 1

        @blk.scalar
        def _(s):
            s.wait_ge(tsem, c.ts)
            s.activation(B["DECS"][:], ps[:, 2560:2564], AF.Exp, scale=-1.0 / 16).then_inc(ssem, 1)
            c.ss += 1
        c.decs_last = c.ss

        @blk.vector
        def _(v):
            v.wait_ge(ssem, c.ss)
            v.wait_ge(tsem, c.ts)
            for g4 in range(4):
                v.drain()
                v.tensor_tensor(B["HS"][:, g4 * 64:(g4 + 1) * 64], B["HS"][:, g4 * 64:(g4 + 1) * 64],
                                ps[:, 2048 + 64 * g4: 2048 + 64 * g4 + 64], ALU.add)
                v.drain()
                ii = v.tensor_scalar(B["HS"][:, g4 * 64:(g4 + 1) * 64], B["HS"][:, g4 * 64:(g4 + 1) * 64],
                                B["DECS"][:, g4:g4 + 1], None, ALU.mult)
            ii.then_inc(vsem, 1)
            c.vs += 1

    def epi(ch):
        @blk.scalar
        def _(s, ch=ch):
            s.wait_ge(tsem, o_done[ch])
            s.wait_ge(vsem, c.rsv)
            s.activation(B["SQS"][:], ps[:, 0:1024], AF.Square).then_inc(ssem, 1)
            c.ss += 1

        @blk.vector
        def _(v):
            v.wait_ge(ssem, c.ss)
            v.tensor_reduce(B["RSTT"][:], bass.AP(B["SQS"], 0, [[1024, 128], [64, 16], [1, 64]]),
                            AX.X, ALU.add)
            v.drain()
            v.tensor_scalar(B["RSTT"][:], B["RSTT"][:], 1.0 / 64, 1e-5, ALU.mult, ALU.add).then_inc(vsem, 1)
            c.vs += 1
        c.rsv = c.vs

        @blk.scalar
        def _(s):
            s.wait_ge(vsem, c.vs)
            s.activation(B["RSTT"][:], B["RSTT"][:], AF.Sqrt).then_inc(ssem, 1)
            c.ss += 1

        @blk.vector
        def _(v, ch=ch):
            v.wait_ge(ssem, c.ss)
            v.reciprocal(B["RSTT"][:], B["RSTT"][:])
            v.drain()
            for hd in range(16):
                ii = v.scalar_tensor_tensor(B["OG"][:, ch * 1024 + 64 * hd: ch * 1024 + 64 * hd + 64],
                                            ps[:, 64 * hd: 64 * hd + 64],
                                            B["RSTT"][:, hd: hd + 1],
                                            B["GT"][:, ch * 1024 + 64 * hd: ch * 1024 + 64 * hd + 64],
                                            ALU.mult, ALU.mult)
            ii.then_inc(vsem, 1)
            c.vs += 1
        og_vals[ch] = c.vs

    def ogt(rnd):
        @blk.tensor
        def _(t, rnd=rnd):
            t.wait_ge(vsem, og_vals[rnd])
            for k in range(8):
                ii = t.transpose(ps[:, 3072 + k * 128: 3072 + k * 128 + 128],
                            B["OG"][:, rnd * 1024 + k * 128: rnd * 1024 + k * 128 + 128], B["EYE"][:])
            ii.then_inc(tsem, 1)
            c.ts += 1

        @blk.vector
        def _(v, rnd=rnd):
            v.wait_ge(tsem, c.ts)
            v.tensor_copy(B["OGF"][:, rnd * 1024:(rnd + 1) * 1024], ps[:, 3072:4096]).then_inc(vsem, 1)
            c.vs += 1

    for ch in range(2):
        qv = front(ch)
        if ch == 1:
            epi(0)
        hdloop(ch, qv)
        if ch == 1:
            ogt(0)
        tail(ch)
    epi(1)
    ogt(1)

    wo_waits = []
    wo_ts0 = c.ts
    wo_mm = [wo_ts0 + vb + 1 for vb in range(8)]

    wsl0 = list(c.wsl)

    @blk.sync
    def _(e):
        for vb in range(8):
            slot = vb % 4
            e.wait_ge(tsem, max(wsl0[2 * slot], wsl0[2 * slot + 1]) if vb < 4
                      else wo_mm[vb - 4])
            sem = sq4[vb % 4]
            if c.sq4[vb % 4]:
                e.wait_ge(sem, c.sq4[vb % 4])
            e.dma_start(out=B["WSLAB"][:, slot * 1024: slot * 1024 + 1024],
                        in_=bass.AP(B["wo1"].tensor, vb * 128 * 1024,
                                    [[1024, 128], [1, 1024]])).then_inc(sem, 16)
            c.sq4[vb % 4] += 16
            wo_waits.append((sem, c.sq4[vb % 4]))
    for i in range(8):
        c.wsl[i] = wo_mm[i // 2 + 4]

    @blk.tensor
    def _(t):
        t.wait_ge(vsem, c.vs)
        for vb in range(8):
            slot = vb % 4
            t.wait_ge(*wo_waits[vb])
            for ch in range(2):
                for dh in range(2):
                    ii = t.matmul(ps[:, (ch * 2 + dh) * 512: (ch * 2 + dh) * 512 + 512],
                             B["OGF"][:, (ch * 8 + vb) * 128: (ch * 8 + vb) * 128 + 128],
                             B["WSLAB"][:, slot * 1024 + dh * 512: slot * 1024 + dh * 512 + 512],
                             start=(vb == 0), stop=(vb == 7))
            ii.then_inc(tsem, 1)
            c.ts += 1
            assert c.ts == wo_mm[vb]

    @blk.vector
    def _(v):
        v.wait_ge(tsem, c.ts)
        for ch in range(2):
            for dh in range(2):
                v.drain()
                xs = X[:, (gch0 + ch) * 1024 + dh * 512: (gch0 + ch) * 1024 + dh * 512 + 512]
                ii = v.tensor_tensor(xs, xs, ps[:, (ch * 2 + dh) * 512: (ch * 2 + dh) * 512 + 512],
                                     ALU.add)
        ii.then_inc(vsem, 1)
        c.vs += 1




def l1_ffn_tt(nc, blk, c, tt, B, sqab, gqab, tsem, vsem, ssem):
    X, ps = B["X"], B["ps"]
    XH, XL = B["XNFH"], B["XNFL"]
    wcs, wvs = B["wcs"], B["wvs"]
    ybs = B["ybs"]
    pend = None

    def w1split(hc, dma_wait, ts_wait, val):
        @blk.scalar
        def _(s, hc=hc):
            s.wait_ge(*dma_wait)
            s.wait_ge(*ts_wait)
            half = (hc % 2) * 1024
            s.copy(B["W1SH"][:, half: half + 1024],
                   B["W1SF"][:, half: half + 1024]).then_inc(wcs, 1)
            c.wc += 1

        @blk.vector
        def _(v, hc=hc):
            v.wait_ge(wcs, c.wc)
            half = (hc % 2) * 1024
            v.tensor_tensor(B["W1SL"][:, half: half + 1024],
                            B["W1SF"][:, half: half + 1024],
                            B["W1SH"][:, half: half + 1024],
                            ALU.subtract).then_inc(wvs, 1)
            c.wv += 1
            assert c.wv == val, (c.wv, val)

    def w2split(hc, dma_wait, rd_wait, val):
        @blk.scalar
        def _(s, hc=hc):
            s.wait_ge(*dma_wait)
            s.wait_ge(*rd_wait)
            half = (hc % 2) * 1024
            s.copy(B["W2SH"][:, half: half + 1024],
                   B["W2SF"][:, half: half + 1024]).then_inc(wcs, 1)
            c.wc += 1

        @blk.vector
        def _(v, hc=hc):
            v.wait_ge(wcs, c.wc)
            half = (hc % 2) * 1024
            v.tensor_tensor(B["W2SL"][:, half: half + 1024],
                            B["W2SF"][:, half: half + 1024],
                            B["W2SH"][:, half: half + 1024],
                            ALU.subtract).then_inc(wvs, 1)
            c.wv += 1
            assert c.wv == val, (c.wv, val)

    for e in range(8):
        @blk.tensor
        def _pre(t, e=e):
            t.wait_ge(ssem, c.ss)
            if e == 0:
                t.wait_ge(vsem, c.vs)
            t.nop().then_inc(tsem, 1)
            c.ts += 1
        a_ts0 = c.ts
        b_ts0 = a_ts0 + 16

        W0 = c.wv
        w1v, w2v = {}, {}
        if pend is None:
            w1v[0], w1v[1] = W0 + 1, W0 + 2
            i = W0 + 2
        else:
            w1v[0], w1v[1] = W0 - 1, W0
            assert (w1v[0], w1v[1]) == tuple(pend["vals"]), (w1v, pend)
            i = W0
        for g in range(16):
            if g + 2 < 16:
                i += 1
                w1v[g + 2] = i
            if g == 12:
                i += 1
                w2v[0] = i
            if g == 13:
                i += 1
                w2v[1] = i
        for g in range(14):
            i += 1
            w2v[g + 2] = i

        w1_dma, w2_dma = [], []

        @blk.sync
        def _(e2, e=e, w1v=w1v, w2v=w2v):
            for hc in range(16):
                e2.wait_ge(wvs, w1v[hc - 2] if hc >= 2 else c.lw1[14 + hc])
                sem = sqab[hc % 2]
                half = (hc % 2) * 1024
                e2.dma_start(out=B["W1SF"][:, half: half + 1024],
                             in_=bass.AP(B["w1"].tensor, (e * 16 + hc) * 131072,
                                         [[1024, 128], [1, 1024]])).then_inc(sem, 16)
                if hc % 2 == 0:
                    c.sqa += 16
                    w1_dma.append((sem, c.sqa))
                else:
                    c.sqb += 16
                    w1_dma.append((sem, c.sqb))
            for hc in range(16):
                e2.wait_ge(wvs, w2v[hc - 2] if hc >= 2 else c.lw2[14 + hc])
                sem = gqab[hc % 2]
                half = (hc % 2) * 1024
                e2.dma_start(out=B["W2SF"][:, half: half + 1024],
                             in_=bass.AP(B["w2"].tensor, (e * 16 + hc) * 131072,
                                         [[1024, 128], [1, 1024]])).then_inc(sem, 16)
                if hc % 2 == 0:
                    c.gqa += 16
                    w2_dma.append((sem, c.gqa))
                else:
                    c.gqb += 16
                    w2_dma.append((sem, c.gqb))

        if pend is not None:
            assert w1_dma[0] == pend["dma"][0] and w1_dma[1] == pend["dma"][1], \
                (w1_dma[:2], pend)

        ss0 = c.ss
        vs0 = c.vs
        if pend is None:
            w1split(0, w1_dma[0], (tsem, a_ts0), w1v[0])
            w1split(1, w1_dma[1], (tsem, a_ts0), w1v[1])
        pend = None
        for hc in range(16):
            @blk.tensor
            def _(t, hc=hc, e=e, w1v=w1v):
                t.wait_ge(wvs, w1v[hc])
                if hc == 0 and e > 0:
                    t.wait_ge(vsem, vs0 - 1)
                if hc >= 2:
                    t.wait_ge(ssem, ss0 + hc - 1)
                ab = (4 + hc % 2) * 512
                half = (hc % 2) * 1024
                for dc in range(8):
                    t.matmul(ps[:, ab: ab + 512], B["W1SH"][:, half + dc * 128: half + dc * 128 + 128],
                             XH[:, dc * 512:(dc + 1) * 512], start=(dc == 0), stop=False)
                    t.matmul(ps[:, ab: ab + 512], B["W1SH"][:, half + dc * 128: half + dc * 128 + 128],
                             XL[:, dc * 512:(dc + 1) * 512], start=False, stop=False)
                    ii = t.matmul(ps[:, ab: ab + 512], B["W1SL"][:, half + dc * 128: half + dc * 128 + 128],
                                  XH[:, dc * 512:(dc + 1) * 512], start=False, stop=(dc == 7))
                ii.then_inc(tsem, 1)
                c.ts += 1

            @blk.scalar
            def _(s, hc=hc, e=e):
                s.wait_ge(tsem, c.ts)
                if hc >= 2:
                    s.wait_ge(vsem, vs0 + hc - 1)
                ab = (4 + hc % 2) * 512
                s.activation(B["HTMP"][:, (hc % 2) * 512: (hc % 2) * 512 + 512],
                             ps[:, ab: ab + 512], AF.Gelu,
                             bias=B["B1S"][:, e * 16 + hc: e * 16 + hc + 1]).then_inc(ssem, 1)
                c.ss += 1

            @blk.vector
            def _(v, hc=hc):
                v.wait_ge(ssem, c.ss)
                ht = B["HTMP"][:, (hc % 2) * 512: (hc % 2) * 512 + 512]
                v.tensor_copy(B["H1FH"][:, hc * 512:(hc + 1) * 512], ht)
                v.drain()
                v.tensor_tensor(B["H1FL"][:, hc * 512:(hc + 1) * 512], ht,
                                B["H1FH"][:, hc * 512:(hc + 1) * 512],
                                ALU.subtract).then_inc(vsem, 1)
                c.vs += 1

            if hc + 2 < 16:
                w1split(hc + 2, w1_dma[hc + 2], (tsem, a_ts0 + hc + 1), w1v[hc + 2])
            if hc == 12:
                w2split(0, w2_dma[0], (ybs, c.lyb), w2v[0])
            if hc == 13:
                w2split(1, w2_dma[1], (ybs, c.lyb), w2v[1])

        for hc in range(16):
            @blk.tensor
            def _(t, hc=hc, e=e, w2v=w2v):
                t.wait_ge(wvs, w2v[hc])
                t.wait_ge(vsem, vs0 + hc + 1)
                half = (hc % 2) * 1024
                for tc in ((3, 0, 1, 2) if hc == 0 else (2, 3, 0, 1)):
                    if hc == 0 and tc == 2:
                        t.wait_ge(ssem, ss0 + 16)
                    for dh in range(2):
                        yb = (tc * 2 + dh) * 512
                        hsl = slice(hc * 512 + tc * 128, hc * 512 + tc * 128 + 128)
                        t.matmul(ps[:, yb: yb + 512], B["H1FH"][:, hsl],
                                 B["W2SH"][:, half + dh * 512: half + dh * 512 + 512],
                                 start=(hc == 0), stop=False)
                        t.matmul(ps[:, yb: yb + 512], B["H1FH"][:, hsl],
                                 B["W2SL"][:, half + dh * 512: half + dh * 512 + 512],
                                 start=False, stop=False)
                        if e == 7 and hc == 15:
                            t.matmul(ps[:, yb: yb + 512], B["ONESR"][:],
                                     B["B2S"][0:1, dh * 512:(dh + 1) * 512],
                                     start=False, stop=False)
                        ii = t.matmul(ps[:, yb: yb + 512], B["H1FL"][:, hsl],
                                      B["W2SH"][:, half + dh * 512: half + dh * 512 + 512],
                                      start=False, stop=(hc == 15))
                        if hc == 15 and tc == 2 and dh == 1:
                            ii.then_inc(ybs, 1)
                            c.yb += 1
                if hc == 15:
                    ii.then_inc(ybs, 1)
                    c.yb += 1
                else:
                    ii.then_inc(tsem, 1)
                    c.ts += 1

            if hc + 2 < 16:
                w2split(hc + 2, w2_dma[hc + 2], (tsem, b_ts0 + hc + 1), w2v[hc + 2])
            if e < 7 and hc in (14, 15):
                hcn = hc - 14
                dw = (sqab[hcn], (c.sqa if hcn == 0 else c.sqb) + 16)
                w1split(hcn, dw, (tsem, a_ts0 + 15 + hcn), c.wv + 1)
                if hcn == 0:
                    pend = {"dma": [dw], "vals": [c.wv]}
                else:
                    pend["dma"].append(dw)
                    pend["vals"].append(c.wv)

        c.lw1 = w1v
        c.lw2 = w2v
        c.lyb = c.yb

        @blk.vector
        def _(v, tt=tt):
            v.wait_ge(ybs, c.yb - 1)
            for tc2, dh in ((2, 0), (2, 1)):
                v.drain()
                tc = tt * 4 + tc2
                xs = X[:, tc * 1024 + dh * 512: tc * 1024 + dh * 512 + 512]
                ii = v.scalar_tensor_tensor(xs, ps[:, (tc2 * 2 + dh) * 512: (tc2 * 2 + dh) * 512 + 512],
                                            B["NORMG"][:, tc: tc + 1], xs, ALU.mult, ALU.add)
            ii.then_inc(vsem, 1)
            c.vs += 1
            v.wait_ge(ybs, c.yb)
            for tc2, dh in ((3, 0), (3, 1), (0, 0), (0, 1), (1, 0), (1, 1)):
                v.drain()
                tc = tt * 4 + tc2
                xs = X[:, tc * 1024 + dh * 512: tc * 1024 + dh * 512 + 512]
                ii = v.scalar_tensor_tensor(xs, ps[:, (tc2 * 2 + dh) * 512: (tc2 * 2 + dh) * 512 + 512],
                                            B["NORMG"][:, tc: tc + 1], xs, ALU.mult, ALU.add)
            ii.then_inc(vsem, 1)
            c.vs += 1




def kernel(**inputs):
    nc = build_program(upto="full")
    in_maps = [prep_core_inputs(inputs, cc) for cc in range(NCORE)]
    res = run_bass_kernel_spmd(nc, in_maps, core_ids=list(range(NCORE)))
    return np.stack([res.results[cc]["out"] for cc in range(NCORE)], axis=0)
